# version 10
# baseline (speedup 1.0000x reference)
import numpy as np
import concourse.bass as bass
import concourse.mybir as mybir
from concourse.bass_utils import run_bass_kernel_spmd
from contextlib import ExitStack

F32 = mybir.dt.float32
BF16 = mybir.dt.bfloat16
AF = mybir.ActivationFunctionType
ALU = mybir.AluOpType
AX = mybir.AxisListType

P = 128
TB = 512
NTC = TB // P
NBLK = 8
D = 1024
NFC = D // P
HID = 4096
HB = 1024
NHK = HB // P
NHBR = HID // HB
NETS = 3
NROUND = NETS * NHBR
NH = 16
NHC = 8
SEQT = 2048
EPS = 1e-5

FC_EPS, FC_NHALF, FC_INV1024, FC_HALF, FC_ONE = 0, 1, 2, 3, 4


def _consts():
    half = 32
    theta = 1.0 / 10000.0 ** (np.arange(half, dtype=np.float64) / half)
    pos = np.arange(SEQT, dtype=np.float64)
    f = pos[:, None] * theta[None, :]
    f = np.concatenate([f, f], axis=1)
    cos = np.cos(f).astype(np.float32)
    sin = np.sin(f).astype(np.float32)
    sgn = np.ones((1, 64), np.float32)
    sgn[0, :half] = -1.0
    sins = sin * sgn

    def tab(a):
        t = a.T
        return np.ascontiguousarray(np.concatenate([t, t], axis=0).astype(np.float32))
    kr = np.arange(P)[:, None]
    qr = np.arange(P)[None, :]
    perm = np.zeros((P, P), np.float32)
    for d in range(P):
        h, dd = d // 64, d % 64
        perm[h * 64 + (dd + 32) % 64, d] = 1.0
    fconst = np.zeros((P, 8), np.float32)
    fconst[:, FC_EPS] = EPS
    fconst[:, FC_NHALF] = -0.5
    fconst[:, FC_INV1024] = 1.0 / 1024.0
    fconst[:, FC_HALF] = 0.5
    fconst[:, FC_ONE] = 1.0
    fconst[:, 5] = -1.0
    return dict(cq=tab(cos) * 0.125, sq=tab(sins) * 0.125, ck=tab(cos),
                sk=tab(sins),
                tri_lead=(qr >= kr).astype(np.float32),
                tri_trail=(qr < kr).astype(np.float32),
                perm=perm, ident=np.eye(P, dtype=np.float32), fconst=fconst,
                ones_row=np.ones((1, P), np.float32),
                ones_col=np.ones((P, 1), np.float32))


class Sem:
    def __init__(self, nc, es, name):
        self.h = es.enter_context(nc.semaphore(name))
        self.n = 0
        self.name = name


class Rec:
    ENGINES = ("sync", "scalar", "vector", "tensor", "gpsimd")

    def __init__(self, nc):
        self.nc = nc
        self.items = {e: [] for e in self.ENGINES}
        self.waited = {e: {} for e in self.ENGINES}

    def set_engine_sems(self, esems):
        self.esems = esems

    def op(self, engine, emit, waits=(), inc=None, inck=1, is_dma=False):
        flat = []
        for w in waits:
            if w is None:
                continue
            if isinstance(w, list):
                flat.extend(x for x in w if x is not None)
            else:
                flat.append(w)
        waits = tuple(flat)
        esem = getattr(self, "esems", {}).get(engine)
        if esem is not None and not is_dma:
            if esem.n > 0:
                waits = waits + ((esem, esem.n),)
            inc, inck = esem, 1
        ev = None
        if inc is not None:
            inc.n += inck
            ev = (inc, inc.n)
        self.items[engine].append((waits, emit, ev, inck))
        return ev

    def set_dma_sems(self, qsems):
        self.qsems = qsems
        self.qord = {e: 0 for e in qsems}

    def dma(self, engine, out, in_, waits=(), inc=None, **kw):
        sems = self.qsems[engine]
        k = self.qord[engine]
        self.qord[engine] = k + 1
        par = k % 2
        sem = sems[par]
        chain = (sem, 16 * (k // 2)) if k >= 2 else None

        def emit(eng, o=out, i=in_, kw=kw):
            return eng.dma_start(out=o, in_=i, **kw)
        return self.op(engine, emit, tuple(waits) + (chain,), sem, 16, is_dma=True)

    def flush(self):
        use = [e for e in self.ENGINES if self.items[e]]
        if not use:
            return
        with self.nc.Block() as blk:
            for e in use:
                items = self.items[e]
                waited = self.waited[e]

                def runner(eng, items=items, waited=waited):
                    for (waits, emit, ev, inck) in items:
                        for (s, c) in waits:
                            if waited.get(id(s), -1) < c:
                                eng.wait_ge(s.h, c)
                                waited[id(s)] = c
                        ins = emit(eng)
                        if ev is not None:
                            ins.then_inc(ev[0].h, inck)
                getattr(blk, e)(runner)
        for e in use:
            self.items[e] = []


def build_nc(n_blocks=NBLK, n_sub=3, name="mk", dbg=None, serial=False, lvl3=False, skew=3, serecip=False):
    nc = bass.Bass("TRN2", target_bir_lowering=False, name=f"{name}_{n_blocks}_{n_sub}")
    C = _consts()

    x_in = nc.dram_tensor("x_in", [NBLK * TB, D], F32, kind="ExternalInput").ap()
    wqkv = nc.dram_tensor("wqkv", [D, 3 * D], F32, kind="ExternalInput").ap()
    wout = nc.dram_tensor("wout", [D, D], F32, kind="ExternalInput").ap()
    gate_w = nc.dram_tensor("gate_w", [2, D], F32, kind="ExternalInput").ap()
    w1 = nc.dram_tensor("w1", [2, D, HID], F32, kind="ExternalInput").ap()
    b1 = nc.dram_tensor("b1", [2, HID], F32, kind="ExternalInput").ap()
    w2 = nc.dram_tensor("w2", [2, HID, D], F32, kind="ExternalInput").ap()
    b2 = nc.dram_tensor("b2", [2, D], F32, kind="ExternalInput").ap()
    sw1 = nc.dram_tensor("sw1", [D, HID], F32, kind="ExternalInput").ap()
    sb1 = nc.dram_tensor("sb1", [HID], F32, kind="ExternalInput").ap()
    sw2 = nc.dram_tensor("sw2", [HID, D], F32, kind="ExternalInput").ap()
    sb2 = nc.dram_tensor("sb2", [D], F32, kind="ExternalInput").ap()
    y = nc.dram_tensor("y", [NBLK * TB, D], F32, kind="ExternalOutput").ap()
    ydbg = nc.dram_tensor("ydbg", [P, 16384], F32, kind="ExternalOutput").ap() if dbg else None

    s_w1 = nc.dram_tensor("s_w1", [NETS * D, HID], BF16, kind="Internal").ap()
    s_w2 = nc.dram_tensor("s_w2", [NETS * HID, D], BF16, kind="Internal").ap()
    s_qkv = nc.dram_tensor("s_qkv", [D, 3 * D], BF16, kind="Internal").ap()
    s_wout = nc.dram_tensor("s_wout", [D, D], BF16, kind="Internal").ap()

    def bft(a):
        return np.ascontiguousarray(a.astype(mybir.dt.np(BF16)))
    c_cq = nc.inline_tensor(C["cq"], name="c_cq").ap()
    c_sq = nc.inline_tensor(C["sq"], name="c_sq").ap()
    c_ck = nc.inline_tensor(C["ck"], name="c_ck").ap()
    c_sk = nc.inline_tensor(C["sk"], name="c_sk").ap()
    c_lead = nc.inline_tensor(bft(C["tri_lead"]), name="c_lead").ap()
    c_trail = nc.inline_tensor(bft(C["tri_trail"]), name="c_trail").ap()
    c_perm = nc.inline_tensor(bft(C["perm"]), name="c_perm").ap()
    c_id_b = nc.inline_tensor(bft(C["ident"]), name="c_id_b").ap()
    c_fc = nc.inline_tensor(C["fconst"], name="c_fc").ap()
    c_or_b = nc.inline_tensor(bft(C["ones_row"]), name="c_or_b").ap()
    c_or_f = nc.inline_tensor(C["ones_row"], name="c_or_f").ap()
    c_oc_b = nc.inline_tensor(bft(C["ones_col"]), name="c_oc_b").ap()
    c_z16 = nc.inline_tensor(np.zeros((1, 16), np.float32), name="c_z16").ap()

    with ExitStack() as es:
        ee = es.enter_context
        LDq = [Sem(nc, es, "LDa"), Sem(nc, es, "LDb")]
        SCq = [Sem(nc, es, "SCa"), Sem(nc, es, "SCb")]
        GDq = [Sem(nc, es, "GDa"), Sem(nc, es, "GDb")]
        PE = Sem(nc, es, "PE")
        SE = Sem(nc, es, "SE")
        VE = Sem(nc, es, "VE")
        PO = Sem(nc, es, "PO")

        x_tm = ee(nc.sbuf_tensor("x_tm", [P, NTC * D], F32))
        kc_sb = ee(nc.sbuf_tensor("kc_sb", [P, NHC * 1024], BF16))
        vc_sb = ee(nc.sbuf_tensor("vc_sb", [P, 8 * 1040], BF16))
        xln_fm = ee(nc.sbuf_tensor("xln_fm", [P, NTC * D], BF16))
        idb = ee(nc.sbuf_tensor("idb", [P, P], BF16))
        permb = ee(nc.sbuf_tensor("permb", [P, P], BF16))
        or_b = ee(nc.sbuf_tensor("or_b", [1, P], BF16))
        or_f = ee(nc.sbuf_tensor("or_f", [1, P], F32))
        oc_b = ee(nc.sbuf_tensor("oc_b", [P, 1], BF16))
        lead_sb = ee(nc.sbuf_tensor("lead_sb", [P, P], BF16))
        trail_sb = ee(nc.sbuf_tensor("trail_sb", [P, P], BF16))
        fc = ee(nc.sbuf_tensor("fc", [P, 8], F32))
        b1s = ee(nc.sbuf_tensor("b1s", [P, NETS * 32], F32))
        gw = ee(nc.sbuf_tensor("gw", [P, NFC * 2], BF16))
        b2b = ee(nc.sbuf_tensor("b2b", [P, D], F32))
        stats = ee(nc.sbuf_tensor("stats", [P, 28], F32))
        negm128 = ee(nc.sbuf_tensor("negm128", [P, 1], F32))
        negmH = ee(nc.sbuf_tensor("negmH", [P, NH], F32))
        xlt = ee(nc.sbuf_tensor("xlt", [P, 2 * D], BF16))
        xsq = ee(nc.sbuf_tensor("xsq", [P, D], F32))
        rows = ee(nc.sbuf_tensor("rows", [1, 4096], F32))
        rows2 = ee(nc.sbuf_tensor("rows2", [1, 128], F32))
        rows_b = ee(nc.sbuf_tensor("rows_b", [1, 16], BF16))
        m0row = ee(nc.sbuf_tensor("m0row", [1, TB], F32))
        g1row = ee(nc.sbuf_tensor("g1row", [1, TB], F32))
        cqs = ee(nc.sbuf_tensor("cqs", [P, TB], F32))
        sqt = ee(nc.sbuf_tensor("sqt", [P, TB], F32))
        cks = ee(nc.sbuf_tensor("cks", [P, TB], F32))
        sks = ee(nc.sbuf_tensor("sks", [P, TB], F32))

        fcell = ee(nc.sbuf_tensor("fcell", [1, 16], F32))

        R = Rec(nc)
        R.set_dma_sems({"sync": LDq, "scalar": SCq, "gpsimd": GDq})
        R.set_engine_sems({"vector": VE, "scalar": SE, "gpsimd": PO})
        _uid = [0]

        e_fcell = [None]

        def fence(engine, inc, waits=()):
            waits = tuple(waits) + (e_fcell[0],)
            """Tiny datapath op on `engine` carrying `inc`: its completion
            implies all prior same-engine ops' writebacks are visible
            (engine issue waits for pipe-empty). Use for cross-engine RAW
            edges within a Block."""
            if engine == "vector":
                emit = lambda v: v.tensor_copy(out=fcell[0:1, 0:1], in_=fcell[0:1, 1:2])
            elif engine == "scalar":
                emit = lambda sc: sc.activation(out=fcell[0:1, 2:3], in_=fcell[0:1, 3:4], func=AF.Copy)
            elif engine == "gpsimd":
                emit = lambda g: g.tensor_copy(out=fcell[0:1, 4:5], in_=fcell[0:1, 5:6])
            else:
                raise ValueError(engine)
            return R.op(engine, emit, waits, inc)

        def un(base):
            _uid[0] += 1
            return f"{base}_{_uid[0]}"

        def rsc(i):
            return rows2[0:1, i:i + 1]

        e_fcell[0] = R.dma("scalar", fcell[:], c_z16)
        e_sw1 = R.dma("gpsimd", s_w1[0:D, :], sw1)
        e_w1 = R.dma("gpsimd", s_w1[D:3 * D, :], w1.rearrange("e k h -> (e k) h"))
        e_sw2 = R.dma("gpsimd", s_w2[0:HID, :], sw2)
        e_w2c = R.dma("gpsimd", s_w2[HID:3 * HID, :], w2.rearrange("e k h -> (e k) h"))
        e_qkvc = R.dma("gpsimd", s_qkv, wqkv)
        e_woutc = R.dma("gpsimd", s_wout, wout)

        ev_c = []
        for dst, src in ((idb, c_id_b), (permb, c_perm), (or_b, c_or_b),
                         (or_f, c_or_f), (oc_b, c_oc_b), (lead_sb, c_lead),
                         (trail_sb, c_trail), (fc, c_fc)):
            ev_c.append(R.dma("scalar", dst[:], src))

        with ExitStack() as ps:
            trp = ps.enter_context(nc.psum_tensor("p_tr0", [P, 512], F32))
            srcs = [sb1[None, :], b1[0:1, :], b1[1:2, :]]
            evmm = None
            for net in range(NETS):
                e_rd = R.dma("scalar", rows[0:1, :], srcs[net],
                             waits=[evmm])
                for hcb in range(32):
                    src = rows[0:1, hcb * P:(hcb + 1) * P]
                    col = net * 32 + hcb
                    evmm = R.op("tensor", (lambda t, s=src, c=col: t.matmul(
                        out=trp[:, c:c + 1], lhsT=s, rhs=fc[0:1, FC_ONE:FC_ONE + 1],
                        start=True, stop=True)),
                        waits=[e_rd, ev_c[-1]], inc=PE)
            e_b1s = R.op("vector", lambda v: v.tensor_copy(out=b1s[:], in_=trp[:, 0:96]),
                         waits=[evmm], inc=VE)
            e_gr0 = R.dma("scalar", rows[0:1, 0:D], gate_w[0:1, :], waits=[evmm])
            e_gr1 = R.dma("scalar", rows[0:1, D:2 * D], gate_w[1:2, :])
            evg = None
            for kcb in range(NFC):
                for eexp in range(2):
                    src = rows[0:1, eexp * D + kcb * P: eexp * D + (kcb + 1) * P]
                    col = 96 + kcb * 2 + eexp
                    evg = R.op("tensor", (lambda t, s=src, c=col: t.matmul(
                        out=trp[:, c:c + 1], lhsT=s, rhs=fc[0:1, FC_ONE:FC_ONE + 1],
                        start=True, stop=True)), waits=[e_gr0, e_gr1], inc=PE)
            e_gw = R.op("vector", lambda v: v.tensor_copy(out=gw[:], in_=trp[:, 96:112]),
                        waits=[evg], inc=VE)
            e_b2a = R.dma("scalar", rows[0:1, 0:D], b2[0:1, :], waits=[evg])
            e_b2c = R.dma("scalar", rows[0:1, D:2 * D], b2[1:2, :])
            e_sb2 = R.dma("scalar", rows[0:1, 2 * D:3 * D], sb2[None, :])
            e_s1 = R.op("vector", lambda v: v.tensor_add(
                out=rows[0:1, 0:D], in0=rows[0:1, 0:D],
                in1=rows[0:1, D:2 * D]), waits=[e_b2a, e_b2c, e_b1s, e_gw], inc=VE)
            e_s2 = R.op("vector", lambda v: v.tensor_add(
                out=rows[0:1, 0:D], in0=rows[0:1, 0:D],
                in1=rows[0:1, 2 * D:3 * D]), waits=[e_sb2], inc=VE)
            ebb = []
            for hh in range(2):
                evb = R.op("tensor", (lambda t, hh=hh: t.matmul(
                    out=trp[:, 0:512], lhsT=or_f[0:1, :],
                    rhs=rows[0:1, hh * 512:(hh + 1) * 512],
                    start=True, stop=True)),
                    waits=[e_s2, (ebb[-1] if ebb else None)], inc=PE)
                ebb.append(R.op("vector", (lambda v, hh=hh: v.tensor_copy(
                    out=b2b[:, hh * 512:(hh + 1) * 512], in_=trp[:, 0:512])),
                    waits=[evb], inc=VE))
            R.flush()

        e_const = ev_c[-1]
        ev_prolog_b2b = ebb[-1]
        st = {"x_ready": None, "x_free": None, "xln_ready": None,
              "xln_free": None, "kmax_valid": False, "lvl3": lvl3,
              "skew": skew, "serecip": serecip}

        def ln_phase():
            with ExitStack() as ps:
                trp = ps.enter_context(nc.psum_tensor(un("p_trl"), [P, D], BF16))
                e_ssq = None
                xw = [st["x_ready"], e_const]
                e_sum = R.op("vector", lambda v: v.tensor_reduce(
                    out=stats[:, 0:4],
                    in_=x_tm[:].rearrange("p (tc d) -> p tc d", tc=NTC),
                    axis=AX.X, op=ALU.add), waits=xw, inc=VE)
                for tc in range(NTC):
                    xs = x_tm[:, tc * D:(tc + 1) * D]
                    e_ssq = R.op("scalar", (lambda s, xs=xs, tc=tc: s.activation(
                        out=xsq[:], in_=xs, func=AF.Square,
                        accum_out=stats[:, 4 + tc:5 + tc])), waits=xw, inc=SE)
                e_m = R.op("vector", lambda v: v.tensor_scalar(
                    out=stats[:, 8:12], in0=stats[:, 0:4], scalar1=1.0 / 1024.0,
                    scalar2=None, op0=ALU.mult), waits=[e_ssq, e_sum], inc=VE)
                e_t = R.op("vector", lambda v: v.tensor_tensor(
                    out=stats[:, 12:16], in0=stats[:, 8:12], in1=stats[:, 8:12],
                    op=ALU.mult), inc=VE)
                e_var = R.op("vector", lambda v: v.scalar_tensor_tensor(
                    out=stats[:, 16:20], in0=stats[:, 4:8], scalar=1.0 / 1024.0,
                    in1=stats[:, 12:16], op0=ALU.mult, op1=ALU.subtract),
                    waits=[e_t], inc=VE)
                e_ln = R.op("scalar", lambda s: s.activation(
                    out=stats[:, 20:24], in_=stats[:, 16:20], func=AF.Ln,
                    bias=fc[:, FC_EPS:FC_EPS + 1]), waits=[e_var], inc=SE)
                R.op("scalar", lambda s: s.activation(
                    out=stats[:, 24:28], in_=stats[:, 20:24], func=AF.Exp,
                    scale=fc[:, FC_NHALF:FC_NHALF + 1]), waits=[e_ln], inc=None)
                e_rstd = fence("scalar", SE)
                slot_free = [None, None]
                e_xf = None
                for tc in range(NTC):
                    slot = tc % 2
                    xs = x_tm[:, tc * D:(tc + 1) * D]
                    dst = xlt[:, slot * D:(slot + 1) * D]
                    wl = [e_rstd, e_m, slot_free[slot]]
                    R.op("vector", (lambda v, xs=xs, dst=dst, tc=tc: v.tensor_scalar(
                        out=dst, in0=xs, scalar1=stats[:, 8 + tc:9 + tc],
                        scalar2=stats[:, 24 + tc:25 + tc],
                        op0=ALU.subtract, op1=ALU.mult)), waits=wl, inc=None)
                    e_app = fence("vector", VE)
                    e_tr = None
                    wl2 = [e_app, e_xf, st["xln_free"] if tc == 0 else None]
                    for fcb in range(NFC):
                        src = xlt[:, slot * D + fcb * P: slot * D + (fcb + 1) * P]
                        e_tr = R.op("tensor", (lambda t, src=src, fcb=fcb: t.transpose(
                            out=trp[:, fcb * P:(fcb + 1) * P], in_=src,
                            identity=idb[:])), waits=wl2, inc=PE)
                        wl2 = []
                    e_xf = R.op("vector", (lambda v, tc=tc: v.tensor_copy(
                        out=xln_fm[:, tc * D:(tc + 1) * D], in_=trp[:])),
                        waits=[e_tr], inc=VE)
                    slot_free[slot] = e_xf
                st["xln_ready"] = e_xf
                st["xln_free"] = None
                if dbg == "ln" and not st.get("dbg_done"):
                    st["dbg_done"] = True
                    R.dma("gpsimd", ydbg[:, 0:28], stats[:], waits=[e_xf])
                    R.dma("gpsimd", ydbg[:, 1024:1024 + 4096], xln_fm[:], waits=[e_xf])
                    R.dma("gpsimd", ydbg[:, 6144:6144 + 1024], x_tm[:, 0:1024], waits=[e_xf])
                R.flush()

        def ffn_phase():
            gate = st["x_ready"]
            ln_phase()
            xv = xln_fm[:].rearrange("p (tc fcb o) -> p tc fcb o", tc=NTC, fcb=NFC)
            with ExitStack() as ps:
                pse = ps.enter_context
                m0_fm = pse(nc.sbuf_tensor(un("m0_fm"), [P, NTC * D], BF16))
                mbb = pse(nc.sbuf_tensor(un("mbb"), [P, TB], BF16))
                h_sb = [pse(nc.sbuf_tensor(un(f"h{i}"), [P, NHK * TB], BF16)) for i in range(2)]
                w1b = [pse(nc.sbuf_tensor(un(f"w1b{i}"), [P, NFC * HB], BF16)) for i in range(2)]
                w2b = [pse(nc.sbuf_tensor(un(f"w2b{i}"), [P, NHK * D], BF16)) for i in range(2)]
                acc = pse(nc.sbuf_tensor(un("acc"), [P, NTC * D], F32))
                zp = [pse(nc.psum_tensor(un(f"zp{i}"), [P, TB], F32)) for i in range(3)]
                op = [pse(nc.psum_tensor(un(f"op{i}"), [P, TB], F32)) for i in range(3)]
                gp = pse(nc.psum_tensor(un("gp"), [1, TB], F32))
                mbp = pse(nc.psum_tensor(un("mbp"), [P, TB], F32))

                m0v = m0_fm[:].rearrange("p (tc fcb o) -> p tc fcb o", tc=NTC, fcb=NFC)
                mbv = mbb[:].rearrange("p (tc o) -> p tc o", tc=NTC)
                xln_ev = st["xln_ready"]

                e_ge = [None, None]
                for eexp in range(2):
                    evm = None
                    wl = [xln_ev, e_gw] if eexp == 0 else [e_g0c]
                    for kcb in range(NFC):
                        lhs = gw[:, kcb * 2 + eexp: kcb * 2 + eexp + 1]
                        evm = R.op("tensor", (lambda t, lhs=lhs, kcb=kcb: t.matmul(
                            out=gp[0:1, 0:TB], lhsT=lhs,
                            rhs=xv[:, :, kcb, :], start=(kcb == 0),
                            stop=(kcb == NFC - 1))), waits=wl, inc=PE)
                        wl = []
                    e_ge[eexp] = evm
                    if eexp == 0:
                        R.op("vector", lambda v: v.tensor_copy(
                            out=g1row[:], in_=gp[0:1, 0:TB]), waits=[evm], inc=None)
                        e_g0c = fence("vector", VE)
                R.op("vector", lambda v: v.tensor_tensor(
                    out=m0row[:], in0=gp[0:1, 0:TB], in1=g1row[:], op=ALU.is_le),
                    waits=[e_ge[1]], inc=None)
                e_m0r = fence("vector", VE)
                e_mbp = R.op("tensor", lambda t: t.matmul(
                    out=mbp[:], lhsT=or_f[0:1, :], rhs=m0row[:], start=True,
                    stop=True), waits=[e_m0r], inc=PE)
                e_mbb = R.op("vector", lambda v: v.tensor_copy(out=mbb[:], in_=mbp[:]),
                             waits=[e_mbp], inc=VE)
                for fcb in range(NFC):
                    R.op("vector", (lambda v, fcb=fcb: v.tensor_tensor(
                        out=m0v[:, :, fcb, :], in0=xv[:, :, fcb, :], in1=mbv,
                        op=ALU.mult)), waits=[e_mbb], inc=None)
                e_m0 = fence("vector", VE)

                rounds = [(net, hb) for net in range(NETS) for hb in range(NHBR)]
                e_w1d = [None] * (NROUND + 2)
                e_w2d = [None] * (NROUND + 2)
                e_zg = {}
                e_relu = {}
                e_outg = {}
                e_add = {}
                e_m1 = None

                def w1_dma(r, waits):
                    net, hb = rounds[r]
                    return R.dma("sync", w1b[r % 2][:].rearrange(
                        "p (kcb h) -> p kcb h", kcb=NFC),
                        s_w1[net * D:(net + 1) * D, hb * HB:(hb + 1) * HB].rearrange(
                            "(kcb p) h -> p kcb h", p=P),
                        waits=waits + [e_sw1, e_w1])

                def w2_dma(r, waits):
                    net, hb = rounds[r]
                    return R.dma("sync", w2b[r % 2][:].rearrange(
                        "p (hk o) -> p hk o", hk=NHK),
                        s_w2[net * HID + hb * HB: net * HID + (hb + 1) * HB, :].rearrange(
                            "(hk p) o -> p hk o", p=P),
                        waits=waits + [e_sw2, e_w2c])

                for r in range(min(2, NROUND)):
                    e_w1d[r] = w1_dma(r, [gate])
                    e_w2d[r] = w2_dma(r, [gate])

                gg = 0

                def emit_z(r):
                    net, hb = rounds[r]
                    if net == 0:
                        rhs, rw = xv, [xln_ev]
                    elif net == 1:
                        rhs, rw = m0v, [e_m0]
                    else:
                        rhs, rw = xv, [st.get("m1ev")]
                    for hc in range(NHK):
                        wl = [e_w1d[r]] + rw
                        if r > 0 and hc == 0 and st.get("lvl3", True):
                            wl.append(st.get("relu_f_prev"))
                        rw = []
                        zg = r * NHK + hc
                        zgw = r * NHK + hc - 2
                        if zgw >= 0:
                            wl.append(e_relu.get((zgw // NHK, zgw % NHK)))
                        evz = None
                        for kcb in range(NFC):
                            lhs = w1b[r % 2][:, kcb * HB + hc * P: kcb * HB + (hc + 1) * P]
                            evz = R.op("tensor", (lambda t, lhs=lhs, rhs=rhs, kcb=kcb, zg=zg: t.matmul(
                                out=zp[zg % 3][:], lhsT=lhs, rhs=rhs[:, :, kcb, :],
                                start=(kcb == 0), stop=(kcb == NFC - 1))),
                                waits=wl, inc=PE)
                            wl = []
                        e_zg[(r, hc)] = evz
                        bcol = net * 32 + hb * NHK + hc
                        wl = [evz, e_b1s]
                        if r >= 2 and hc == 0:
                            wl.append(e_zg.get((r - 1, 0)))
                        e_relu[(r, hc)] = R.op("scalar", (lambda s, hc=hc, r=r, bcol=bcol, zg=zg: s.activation(
                            out=h_sb[r % 2][:, hc * TB:(hc + 1) * TB],
                            in_=zp[zg % 3][:], func=AF.Relu,
                            bias=b1s[:, bcol:bcol + 1])), waits=wl, inc=SE)
                    if net == 0 and hb == NHBR - 1:
                        for fcb in range(NFC):
                            R.op("vector", (lambda v, fcb=fcb: v.tensor_tensor(
                                out=xv[:, :, fcb, :], in0=xv[:, :, fcb, :],
                                in1=m0v[:, :, fcb, :], op=ALU.subtract)),
                                waits=[e_zg[(r, NHK - 1)], e_m0], inc=None)
                        st["m1ev"] = fence("vector", VE)
                    if r + 2 < NROUND:
                        e_w1d[r + 2] = w1_dma(r + 2, [e_zg[(r, NHK - 1)]])
                    st["relu_f_prev"] = fence("scalar", SE)

                def emit_out(r):
                    nonlocal gg
                    gi = 0
                    for tc in range(NTC):
                        for half in range(2):
                            wl = [relu_fr[r], e_w2d[r], e_add.get(gg - 2)]
                            evo = None
                            for hk in range(NHK):
                                lhs = h_sb[r % 2][:, hk * TB + tc * P: hk * TB + (tc + 1) * P]
                                rhsw = w2b[r % 2][:, hk * D + half * 512: hk * D + (half + 1) * 512]
                                evo = R.op("tensor", (lambda t, lhs=lhs, rhsw=rhsw, hk=hk, g=gg: t.matmul(
                                    out=op[g % 3][:], lhsT=lhs, rhs=rhsw,
                                    start=(hk == 0), stop=(hk == NHK - 1))),
                                    waits=wl, inc=PE)
                                wl = []
                            e_outg[(r, gi)] = evo
                            accs = acc[:, tc * D + half * 512: tc * D + (half + 1) * 512]
                            src1 = b2b[:, half * 512:(half + 1) * 512] if r == 0 else accs
                            e_add[gg] = R.op("vector", (lambda v, accs=accs, src1=src1, g=gg: v.tensor_tensor(
                                out=accs, in0=op[g % 3][:], in1=src1, op=ALU.add)),
                                waits=[evo, ev_prolog_b2b, e_add.get(gg - 8)], inc=VE)
                            gg += 1
                            gi += 1
                    if r + 2 < NROUND:
                        e_w2d[r + 2] = w2_dma(r + 2, [e_outg[(r, NTC * 2 - 1)]])

                relu_fr = {}
                for r in range(NROUND):
                    emit_z(r)
                    relu_fr[r] = st["relu_f_prev"]
                    if r >= 1:
                        emit_out(r - 1)
                emit_out(NROUND - 1)

                e_epi = None
                last_add = e_add[gg - 1]
                for tc in range(NTC):
                    xs = x_tm[:, tc * D:(tc + 1) * D]
                    e_h = R.op("vector", (lambda v, tc=tc: v.tensor_scalar(
                        out=acc[:, tc * D:(tc + 1) * D],
                        in0=acc[:, tc * D:(tc + 1) * D], scalar1=0.5,
                        scalar2=None, op0=ALU.mult)),
                        waits=[last_add], inc=VE)
                    e_epi = R.op("vector", (lambda v, xs=xs, tc=tc: v.tensor_tensor(
                        out=xs, in0=acc[:, tc * D:(tc + 1) * D], in1=xs,
                        op=ALU.add)), waits=[e_h, gate], inc=VE)
                st["x_ready"] = e_epi
                st["xln_free"] = e_zg[(NROUND - 1, NHK - 1)]
                R.flush()

        def attn_phase(b):
            gate = st["x_ready"]
            brel = b % 4
            pos0 = brel * TB
            curk = pos0 % 1024
            ln_phase()
            xv = xln_fm[:].rearrange("p (tc fcb o) -> p tc fcb o", tc=NTC, fcb=NFC)
            xln_ev = st["xln_ready"]
            e_cs = []
            for dst, src in ((cqs, c_cq), (sqt, c_sq), (cks, c_ck), (sks, c_sk)):
                e_cs.append(R.dma("scalar", dst[:], src[:, pos0:pos0 + TB]))

            with ExitStack() as outer:
                q_fm = outer.enter_context(nc.sbuf_tensor(un("q_fm"), [P, NHC * TB], BF16))
                af = outer.enter_context(nc.sbuf_tensor(un("af"), [P, NHC * TB], BF16))
                qtmp = outer.enter_context(nc.sbuf_tensor(un("qtmp"), [P, TB], BF16))
                sqtmp = outer.enter_context(nc.sbuf_tensor(un("sqtmp"), [P, 2 * TB], BF16))
                woutb = outer.enter_context(nc.sbuf_tensor(un("woutb"), [P, NFC * D], BF16))

                with ExitStack() as ps:
                    pse = ps.enter_context
                    qkvw = [pse(nc.sbuf_tensor(un(f"qkvw{i}"), [P, NFC * 2 * P], BF16)) for i in range(2)]
                    wv_sb = pse(nc.sbuf_tensor(un("wv_sb"), [P, NFC * D], BF16))
                    rt0 = pse(nc.sbuf_tensor(un("rt0"), [P, TB], F32))
                    rt1 = pse(nc.sbuf_tensor(un("rt1"), [P, TB], F32))
                    pa = pse(nc.psum_tensor(un("pa"), [P, TB], F32))
                    pb = pse(nc.psum_tensor(un("pb"), [P, TB], F32))
                    pc = pse(nc.psum_tensor(un("pc"), [P, TB], F32))
                    nr = [pse(nc.psum_tensor(un(f"nr{i}"), [1, TB], F32)) for i in range(3)]
                    nr3 = pse(nc.psum_tensor(un("nr3"), [P, 16], F32))

                    e_wv = R.dma("sync", wv_sb[:].rearrange("p (kcb o) -> p kcb o", kcb=NFC),
                                 s_qkv[:, 2 * D:3 * D].rearrange("(kcb p) o -> p kcb o", p=P),
                                 waits=[e_qkvc, gate])
                    sq4 = s_qkv.rearrange("(kcb p) (m o) -> p kcb m o", p=P, m=3)
                    e_qkd = [None] * NHC

                    def qk_dma(hc, waits):
                        dv = qkvw[hc % 2][:].rearrange(
                            "p (kcb m o) -> p kcb m o", kcb=NFC, m=2)
                        ev1 = R.dma("sync", dv[:, :, 0, :], sq4[:, :, 0, hc * P:(hc + 1) * P],
                                    waits=waits + [e_qkvc, gate])
                        ev2 = R.dma("sync", dv[:, :, 1, :],
                                    sq4[:, :, 1, hc * P:(hc + 1) * P])
                        return [ev1, ev2]
                    for hc in range(min(2, NHC)):
                        e_qkd[hc] = qk_dma(hc, [])

                    e_rope_q = [None] * NHC
                    e_rope_k = [None] * NHC
                    pa_free = None
                    pb_free = None
                    qn_free = [None, None]
                    qn_red = [None, None]
                    e_kn = None
                    for hc in range(NHC):
                        wv4 = qkvw[hc % 2][:].rearrange("p (kcb m o) -> p kcb m o", kcb=NFC, m=2)
                        evq = None
                        wl = [e_qkd[hc], xln_ev, pa_free]
                        for kcb in range(NFC):
                            evq = R.op("tensor", (lambda t, wv4=wv4, kcb=kcb: t.matmul(
                                out=pa[:], lhsT=wv4[:, kcb, 0, :], rhs=xv[:, :, kcb, :],
                                start=(kcb == 0), stop=(kcb == NFC - 1))), waits=wl, inc=PE)
                            wl = []
                        R.op("vector", lambda v: v.tensor_copy(
                            out=qtmp[:], in_=pa[:]), waits=[evq], inc=None)
                        e_qt = fence("vector", VE)
                        e_rot = R.op("tensor", lambda t: t.matmul(
                            out=pb[:], lhsT=permb[:], rhs=qtmp[:], start=True,
                            stop=True), waits=[e_qt, pb_free, e_const], inc=PE)
                        e_c0 = R.op("vector", lambda v: v.tensor_tensor(
                            out=rt0[:], in0=pa[:], in1=cqs[:], op=ALU.mult),
                            waits=[evq, e_cs[0]], inc=VE)
                        e_c1 = R.op("vector", lambda v: v.tensor_tensor(
                            out=rt1[:], in0=pb[:], in1=sqt[:], op=ALU.mult),
                            waits=[e_rot, e_cs[1]], inc=VE)
                        e_rope_q[hc] = R.op("vector", (lambda v, hc=hc: v.tensor_tensor(
                            out=q_fm[:, hc * TB:(hc + 1) * TB], in0=rt0[:],
                            in1=rt1[:], op=ALU.add)), inc=VE)
                        R.op("scalar", (lambda sc, hc=hc: sc.activation(
                            out=sqtmp[:, 0:TB], in_=q_fm[:, hc * TB:(hc + 1) * TB],
                            func=AF.Square)), waits=[e_rope_q[hc], qn_free[0], qn_free[1]], inc=None)
                        e_sqq = fence("scalar", SE)
                        for half in range(2):
                            hh = 2 * hc + half
                            evn = R.op("tensor", (lambda t, half=half: t.matmul(
                                out=nr[half][0:1, 0:TB],
                                lhsT=oc_b[half * 64:half * 64 + 64, 0:1],
                                rhs=sqtmp[half * 64:half * 64 + 64, 0:TB],
                                start=True, stop=True)),
                                waits=[e_sqq, qn_red[half]], inc=PE)
                            R.op("vector", (lambda v, hh=hh, half=half: v.tensor_reduce(
                                out=rows2[0:1, 16 + hh:17 + hh], in_=nr[half][0:1, 0:TB],
                                axis=AX.X, op=ALU.max)), waits=[evn], inc=None)
                            qn_red[half] = fence("vector", VE)
                        qn_free[0] = evn
                        evk = None
                        wl = [e_c0]
                        for kcb in range(NFC):
                            evk = R.op("tensor", (lambda t, wv4=wv4, kcb=kcb: t.matmul(
                                out=pa[:], lhsT=wv4[:, kcb, 1, :], rhs=xv[:, :, kcb, :],
                                start=(kcb == 0), stop=(kcb == NFC - 1))), waits=wl, inc=PE)
                            wl = []
                        R.op("vector", lambda v: v.tensor_copy(
                            out=qtmp[:], in_=pa[:]), waits=[evk, e_rope_q[hc]], inc=None)
                        e_kt = fence("vector", VE)
                        e_rotk = R.op("tensor", lambda t: t.matmul(
                            out=pb[:], lhsT=permb[:], rhs=qtmp[:], start=True,
                            stop=True), waits=[e_kt, e_c1], inc=PE)
                        e_c0k = R.op("vector", lambda v: v.tensor_tensor(
                            out=rt0[:], in0=pa[:], in1=cks[:], op=ALU.mult),
                            waits=[evk, e_cs[2], e_rope_q[hc]], inc=VE)
                        e_c1k = R.op("vector", lambda v: v.tensor_tensor(
                            out=rt1[:], in0=pb[:], in1=sks[:], op=ALU.mult),
                            waits=[e_rotk, e_cs[3]], inc=VE)
                        e_rope_k[hc] = R.op("vector", (lambda v, hc=hc: v.tensor_tensor(
                            out=kc_sb[:, hc * 1024 + curk: hc * 1024 + curk + TB],
                            in0=rt0[:], in1=rt1[:], op=ALU.add)), inc=VE)
                        R.op("scalar", (lambda sc, hc=hc: sc.activation(
                            out=sqtmp[:, TB:2 * TB],
                            in_=kc_sb[:, hc * 1024 + curk: hc * 1024 + curk + TB],
                            func=AF.Square)), waits=[e_rope_k[hc], qn_free[1]], inc=None)
                        e_sqk = fence("scalar", SE)
                        e_kn = R.op("tensor", (lambda t, hc=hc: t.matmul(
                            out=nr[2][0:1, 0:TB], lhsT=oc_b[:], rhs=sqtmp[:, TB:2 * TB],
                            start=(hc == 0), stop=(hc == NHC - 1))),
                            waits=[e_sqk], inc=PE)
                        qn_free[1] = e_kn
                        pa_free = e_c1k
                        pb_free = e_rope_k[hc]
                        if hc + 2 < NHC:
                            e_qkd[hc + 2] = qk_dma(hc + 2, [evk])

                    e_vlast_ve = None
                    e_vlast_pe = None
                    for tc in range(NTC):
                        vslot = ((pos0 // P) + tc) % 8
                        R.op("gpsimd", (lambda g, vslot=vslot: g.memset(
                            vc_sb[:, vslot * 1040:(vslot + 1) * 1040].rearrange(
                                "p (h c) -> p h c", h=NH)[:, :, 64:65], 1.0)),
                            waits=[st.get("vslot_free")])
                        for half in range(2):
                            evv = None
                            wl = [e_wv, xln_ev, e_vlast_ve]
                            for kcb in range(NFC):
                                lhs = xv[:, tc, kcb, :]
                                rhsw = wv_sb[:, kcb * D + half * 512: kcb * D + (half + 1) * 512]
                                evv = R.op("tensor", (lambda t, lhs=lhs, rhsw=rhsw, kcb=kcb: t.matmul(
                                    out=pc[:], lhsT=lhs, rhs=rhsw,
                                    start=(kcb == 0), stop=(kcb == NFC - 1))),
                                    waits=wl, inc=PE)
                                wl = []
                            e_vlast_pe = evv
                            dstv = vc_sb[:, vslot * 1040:(vslot + 1) * 1040].rearrange(
                                "p (h c) -> p h c", h=NH)[:, half * 8:(half + 1) * 8, 0:64]
                            R.op("vector", (lambda v, dstv=dstv: v.tensor_copy(
                                out=dstv, in_=pc[:].rearrange("p (h c) -> p h c", h=8))),
                                waits=[evv], inc=None)
                            e_vlast_ve = fence("vector", VE)

                    e_km = R.op("vector", lambda v: v.tensor_reduce(
                        out=rsc(1), in_=nr[2][0:1, 0:TB], axis=AX.X, op=ALU.max),
                        waits=[e_kn, qn_red[0], qn_red[1]], inc=VE)
                    if brel > 0 and st["kmax_valid"]:
                        e_km = R.op("vector", lambda v: v.tensor_tensor(
                            out=rsc(1), in0=rsc(1), in1=rsc(2), op=ALU.max), inc=VE)
                    R.op("vector", lambda v: v.tensor_copy(out=rsc(2), in_=rsc(1)), inc=None)
                    e_kst = fence("vector", VE)
                    st["kmax_valid"] = True
                    e_l1 = R.op("scalar", lambda sc: sc.activation(
                        out=rows2[0:1, 32:48], in_=rows2[0:1, 16:32], func=AF.Ln),
                        waits=[e_kst], inc=SE)
                    R.op("scalar", lambda sc: sc.activation(
                        out=rsc(4), in_=rsc(1), func=AF.Ln), waits=[e_kst], inc=None)
                    e_l2 = fence("scalar", SE)
                    R.op("vector", lambda v: v.tensor_scalar(
                        out=rows2[0:1, 48:64], in0=rows2[0:1, 32:48],
                        scalar1=rsc(4), scalar2=None, op0=ALU.add),
                        waits=[e_l1, e_l2], inc=None)
                    e_ls = fence("vector", VE)
                    R.op("scalar", lambda sc: sc.activation(
                        out=rows2[0:1, 32:48], in_=rows2[0:1, 48:64], func=AF.Exp,
                        scale=fc[0:1, FC_HALF:FC_HALF + 1]), waits=[e_ls], inc=None)
                    e_M = fence("scalar", SE)
                    R.op("vector", lambda v: v.tensor_scalar(
                        out=rows2[0:1, 48:64], in0=rows2[0:1, 32:48], scalar1=-1.0,
                        scalar2=None, op0=ALU.mult), waits=[e_M], inc=None)
                    e_nM = fence("vector", VE)
                    e_nMb = R.op("tensor", lambda t: t.matmul(
                        out=nr3[:, 0:16], lhsT=or_f[0:1, :], rhs=rows2[0:1, 48:64],
                        start=True, stop=True), waits=[e_nM], inc=PE)
                    e_negm = R.op("vector", lambda v: v.tensor_copy(
                        out=negmH[:], in_=nr3[:, 0:16]), waits=[e_nMb], inc=VE)
                    st["pa_free"] = pa_free
                    st["pb_free"] = pb_free
                    st["xln_free"] = e_vlast_pe
                    R.flush()

                chunks = []
                for ac in range(max(0, pos0 // P - 4), pos0 // P + 4):
                    K0 = ac * P
                    ql = max(0, K0 - pos0)
                    qh = min(TB, K0 + 640 - pos0)
                    lead = (K0 >= pos0)
                    chunks.append(dict(kcol=K0 % 1024, ql=ql, qh=qh, lead=lead,
                                       mq0=(ql if lead else qh - P),
                                       vslot=ac % 8))
                with ExitStack() as ps:
                    pse = ps.enter_context
                    probs = pse(nc.sbuf_tensor(un("probs"), [P, 8 * TB], BF16))
                    recip_sb = pse(nc.sbuf_tensor(un("recip_sb"), [1, 2 * TB], F32))
                    bcast_sb = pse(nc.sbuf_tensor(un("bcast_sb"), [P, TB], F32))
                    sbank = [pse(nc.psum_tensor(un(f"sbank{i}"), [P, TB], F32)) for i in range(3)]
                    outp = [pse(nc.psum_tensor(un(f"outp{i}"), [P, TB], F32)) for i in range(2)]
                    bcp = pse(nc.psum_tensor(un("bcp"), [P, TB], F32))

                    e_wd = R.dma("sync", woutb[:].rearrange("p (kcb o) -> p kcb o", kcb=NFC),
                                 s_wout.rearrange("(kcb p) o -> p kcb o", p=P),
                                 waits=[e_woutc, st.get("woutb_free")])
                    slot_free = [None] * 8
                    bank_free = {0: st["pa_free"], 1: st["pb_free"], 2: e_vlast_ve}
                    head_free = [qn_red[0], qn_red[1]]
                    bcp_free = e_negm
                    cc = 0
                    e_pair_ev = [None] * NH
                    e_om_last = None

                    def emit_chunk(h, ci, ch):
                        nonlocal cc
                        hc = h // 2
                        hrow = (h % 2) * 64
                        bk = sbank[cc % 3]
                        N = ch["qh"] - ch["ql"]
                        slot = cc % 8
                        klhs = kc_sb[hrow:hrow + 64,
                                     hc * 1024 + ch["kcol"]: hc * 1024 + ch["kcol"] + P]
                        qrhs = q_fm[hrow:hrow + 64,
                                    hc * TB + ch["ql"]: hc * TB + ch["qh"]]
                        ev_s = R.op("tensor", (lambda t, klhs=klhs, qrhs=qrhs, bk=bk, N=N: t.matmul(
                            out=bk[:, 0:N], lhsT=klhs, rhs=qrhs, start=True,
                            stop=True)),
                            waits=[bank_free[cc % 3], e_rope_k[hc], e_rope_q[hc]],
                            inc=PE)
                        if ch["ql"] > 0:
                            R.op("gpsimd", (lambda g, slot=slot, ch=ch: g.memset(
                                probs[:, slot * TB: slot * TB + ch["ql"]], 0)),
                                waits=[slot_free[slot]])
                        if ch["qh"] < TB:
                            R.op("gpsimd", (lambda g, slot=slot, ch=ch: g.memset(
                                probs[:, slot * TB + ch["qh"]: (slot + 1) * TB], 0)),
                                waits=[slot_free[slot]])
                        e_z0 = fence("gpsimd", PO)
                        pr = probs[:, slot * TB + ch["ql"]: slot * TB + ch["qh"]]
                        R.op("scalar", (lambda sc, pr=pr, bk=bk, N=N, h=h: sc.activation(
                            out=pr, in_=bk[:, 0:N], func=AF.Exp,
                            bias=negmH[:, h:h + 1])),
                            waits=[ev_s, e_negm, slot_free[slot]], inc=None)
                        ev_e = fence("scalar", SE)
                        bank_free[cc % 3] = ev_e
                        probs_ready = ev_e
                        mq0 = ch["mq0"]
                        if 0 <= mq0 - ch["ql"]:
                            mslice = probs[:, slot * TB + mq0: slot * TB + mq0 + P]
                            tri = lead_sb if ch["lead"] else trail_sb
                            R.op("gpsimd", (lambda g, mslice=mslice, tri=tri: g.tensor_tensor(
                                out=mslice, in0=mslice, in1=tri[:],
                                op=ALU.mult)), waits=[ev_e, e_const], inc=None)
                            probs_ready = fence("gpsimd", PO)
                        cc += 1
                        return dict(slot=slot, ready=probs_ready, z0=e_z0, ch=ch, ci=ci)

                    def emit_consume(h, cx):
                        nonlocal e_om_last
                        ob = outp[h % 2]
                        ch = cx["ch"]
                        ci = cx["ci"]
                        slot = cx["slot"]
                        vlhs = vc_sb[:, ch["vslot"] * 1040 + h * 65:
                                     ch["vslot"] * 1040 + h * 65 + 65]
                        prr = probs[:, slot * TB: (slot + 1) * TB]
                        wl = [cx["ready"], cx["z0"], e_vlast_ve]
                        if ci == 0:
                            wl.append(head_free[h % 2])
                        e_om = R.op("tensor", (lambda t, vlhs=vlhs, prr=prr, ob=ob, ci=ci: t.matmul(
                            out=ob[0:65, :], lhsT=vlhs, rhs=prr, start=(ci == 0),
                            stop=(ci == len(chunks) - 1))),
                            waits=wl, inc=PE)
                        slot_free[slot] = e_om
                        e_om_last = e_om
                        return e_om

                    def emit_tail(h, e_om):
                        nonlocal bcp_free
                        ob = outp[h % 2]
                        if st.get("serecip", True):
                            R.op("vector", (lambda v, ob=ob: v.tensor_copy(
                                out=recip_sb[0:1, 0:TB], in_=ob[64:65, :])),
                                waits=[e_om, st.get("dbgrecip_ev")], inc=None)
                            e_cp = fence("vector", VE)
                            if dbg == "recip" and not st.get("dbg_done2"):
                                st["dbg_done2"] = True
                                st["dbgden_ev"] = R.dma("gpsimd", ydbg[1:2, 0:512],
                                                        recip_sb[0:1, 0:TB], waits=[e_cp])
                            R.op("scalar", lambda sc: sc.activation(
                                out=recip_sb[0:1, TB:2 * TB], in_=recip_sb[0:1, 0:TB],
                                func=AF.Ln), waits=[e_cp], inc=None)
                            e_ln2 = fence("scalar", SE, waits=[st.get("dbgden_ev")] if dbg == "recip" else [])
                            R.op("vector", lambda v: v.tensor_scalar(
                                out=recip_sb[0:1, 0:TB], in0=recip_sb[0:1, TB:2 * TB],
                                scalar1=-1.0, scalar2=None, op0=ALU.mult),
                                waits=[e_ln2], inc=None)
                            e_ng = fence("vector", VE)
                            R.op("scalar", lambda sc: sc.activation(
                                out=recip_sb[0:1, TB:2 * TB], in_=recip_sb[0:1, 0:TB],
                                func=AF.Exp), waits=[e_ng], inc=None)
                            e_recip = fence("scalar", SE)
                        else:
                            R.op("vector", (lambda v, ob=ob: v.reciprocal(
                                out=recip_sb[0:1, 0:TB], in_=ob[64:65, :])),
                                waits=[e_om], inc=None)
                            e_recip = fence("vector", VE)
                        evb = R.op("tensor", (lambda t: t.matmul(
                            out=bcp[0:64, :], lhsT=or_f[0:1, 0:64],
                            rhs=recip_sb[0:1, TB:2 * TB] if st.get("serecip", True) else recip_sb[0:1, 0:TB],
                            start=True, stop=True)),
                            waits=[e_recip, bcp_free], inc=PE)
                        R.op("vector", lambda v: v.tensor_copy(
                            out=bcast_sb[0:64, :], in_=bcp[0:64, :]), waits=[evb], inc=None)
                        e_bc = fence("vector", VE)
                        hrow = (h % 2) * 64
                        hc = h // 2
                        R.op("vector", (lambda v, hrow=hrow, hc=hc, ob=ob: v.tensor_tensor(
                            out=af[hrow:hrow + 64, hc * TB:(hc + 1) * TB],
                            in0=ob[0:64, :], in1=bcast_sb[0:64, :], op=ALU.mult)),
                            waits=[e_bc], inc=None)
                        e_ev = fence("vector", VE)
                        e_pair_ev[h] = e_ev
                        head_free[h % 2] = e_ev
                        bcp_free = e_ev
                        if dbg == "recip" and not st.get("dbg_done"):
                            st["dbg_done"] = True
                            st["dbgrecip_ev"] = R.dma("gpsimd", ydbg[0:1, 0:1024],
                                                      recip_sb[0:1, 0:1024],
                                                      waits=[e_recip])


                    pend = []
                    from collections import deque
                    skew = st.get("skew", 2)
                    for hp in range(0, NH, 2):
                        qs = [deque(), deque()]
                        e_oms = [None, None]
                        for ci in range(len(chunks)):
                            for j in (0, 1):
                                qs[j].append(emit_chunk(hp + j, ci, chunks[ci]))
                            if pend and ci == 1:
                                for pe_ in pend:
                                    emit_tail(*pe_)
                                pend = []
                            for j in (0, 1):
                                if len(qs[j]) > skew:
                                    e_oms[j] = emit_consume(hp + j, qs[j].popleft())
                        for j in (0, 1):
                            while qs[j]:
                                e_oms[j] = emit_consume(hp + j, qs[j].popleft())
                        pend = [(hp, e_oms[0]), (hp + 1, e_oms[1])]
                    for pe_ in pend:
                        emit_tail(*pe_)
                    st["vslot_free"] = e_om_last
                    R.flush()

                with ExitStack() as ps:
                    pse = ps.enter_context
                    o_ps = [pse(nc.psum_tensor(un(f"ops{i}"), [P, TB], F32)) for i in range(2)]
                    gg2 = 0
                    e_res = None
                    e_addw = {}
                    for tc in range(NTC):
                        for half in range(2):
                            wl = [e_wd, e_pair_ev[NH - 1], e_addw.get(gg2 - 2)]
                            evo = None
                            for hcb in range(NFC):
                                lhs = af[:, hcb * TB + tc * P: hcb * TB + (tc + 1) * P]
                                rhsw = woutb[:, hcb * D + half * 512: hcb * D + (half + 1) * 512]
                                evo = R.op("tensor", (lambda t, lhs=lhs, rhsw=rhsw, g=gg2, hcb=hcb: t.matmul(
                                    out=o_ps[g % 2][:], lhsT=lhs, rhs=rhsw,
                                    start=(hcb == 0), stop=(hcb == NFC - 1))),
                                    waits=wl, inc=PE)
                                wl = []
                            xs = x_tm[:, tc * D + half * 512: tc * D + (half + 1) * 512]
                            e_res = R.op("vector", (lambda v, xs=xs, g=gg2: v.tensor_tensor(
                                out=xs, in0=o_ps[g % 2][:], in1=xs,
                                op=ALU.add)), waits=[evo, gate], inc=VE)
                            e_addw[gg2] = e_res
                            gg2 += 1
                    st["x_ready"] = e_res
                    st["woutb_free"] = e_res
                    R.flush()

        for b in range(n_blocks):
            e_xl = R.dma("scalar", x_tm[:].rearrange("p (tc d) -> p tc d", tc=NTC),
                         x_in[b * TB:(b + 1) * TB, :].rearrange("(tc p) d -> p tc d", p=P),
                         waits=[st["x_free"]])
            st["x_ready"] = e_xl
            ffn_phase()
            if n_sub >= 2:
                attn_phase(b)
            if n_sub >= 3:
                ffn_phase()
            e_stw = R.dma("scalar", y[b * TB:(b + 1) * TB, :].rearrange("(tc p) d -> p tc d", p=P),
                          x_tm[:].rearrange("p (tc d) -> p tc d", tc=NTC),
                          waits=[st["x_ready"]])
            st["x_free"] = e_stw
            st["x_ready"] = None
            R.flush()

        R.op("sync", lambda s: s.nop(), waits=[st["x_free"]])
        R.flush()
    return nc


def kernel(L0_attn_x, L0_attn_Wqkv, L0_attn_Wout, L0_ffn_gate_w, L0_ffn_w1,
           L0_ffn_b1, L0_ffn_w2, L0_ffn_b2, L0_ffn_sw1, L0_ffn_sb1,
           L0_ffn_sw2, L0_ffn_sb2):
    x = np.ascontiguousarray(np.asarray(L0_attn_x, dtype=np.float32))
    B, T, Dm = x.shape
    n_cores = 8
    shards = x.reshape(n_cores, (B // n_cores) * T, Dm)
    nc = build_nc()
    ca = np.ascontiguousarray
    w = dict(
        wqkv=ca(np.asarray(L0_attn_Wqkv, np.float32)),
        wout=ca(np.asarray(L0_attn_Wout, np.float32)),
        gate_w=ca(np.asarray(L0_ffn_gate_w, np.float32)),
        w1=ca(np.asarray(L0_ffn_w1, np.float32)),
        b1=ca(np.asarray(L0_ffn_b1, np.float32)),
        w2=ca(np.asarray(L0_ffn_w2, np.float32)),
        b2=ca(np.asarray(L0_ffn_b2, np.float32)),
        sw1=ca(np.asarray(L0_ffn_sw1, np.float32)),
        sb1=ca(np.asarray(L0_ffn_sb1, np.float32)),
        sw2=ca(np.asarray(L0_ffn_sw2, np.float32)),
        sb2=ca(np.asarray(L0_ffn_sb2, np.float32)),
    )
    in_maps = [dict(x_in=ca(shards[c]), **w) for c in range(n_cores)]
    res = run_bass_kernel_spmd(nc, in_maps, core_ids=list(range(n_cores)))
    out = np.concatenate([r["y"] for r in res.results], axis=0)
    return out.reshape(B, T, Dm)


# version 11 (final)
# speedup vs baseline: 1.0670x; 1.0670x over previous
import numpy as np
import concourse.bass as bass
import concourse.mybir as mybir
from concourse.bass_utils import run_bass_kernel_spmd
from contextlib import ExitStack

F32 = mybir.dt.float32
BF16 = mybir.dt.bfloat16
AF = mybir.ActivationFunctionType
ALU = mybir.AluOpType
AX = mybir.AxisListType

P = 128
TB = 512
NTC = TB // P
NBLK = 8
D = 1024
NFC = D // P
HID = 4096
HB = 1024
NHK = HB // P
NHBR = HID // HB
NETS = 3
NROUND = NETS * NHBR
NH = 16
NHC = 8
SEQT = 2048
EPS = 1e-5

FC_EPS, FC_NHALF, FC_INV1024, FC_HALF, FC_ONE = 0, 1, 2, 3, 4


def _consts():
    half = 32
    theta = 1.0 / 10000.0 ** (np.arange(half, dtype=np.float64) / half)
    pos = np.arange(SEQT, dtype=np.float64)
    f = pos[:, None] * theta[None, :]
    f = np.concatenate([f, f], axis=1)
    cos = np.cos(f).astype(np.float32)
    sin = np.sin(f).astype(np.float32)
    sgn = np.ones((1, 64), np.float32)
    sgn[0, :half] = -1.0
    sins = sin * sgn

    def tab(a):
        t = a.T
        return np.ascontiguousarray(np.concatenate([t, t], axis=0).astype(np.float32))
    kr = np.arange(P)[:, None]
    qr = np.arange(P)[None, :]
    perm = np.zeros((P, P), np.float32)
    for d in range(P):
        h, dd = d // 64, d % 64
        perm[h * 64 + (dd + 32) % 64, d] = 1.0
    fconst = np.zeros((P, 8), np.float32)
    fconst[:, FC_EPS] = EPS
    fconst[:, FC_NHALF] = -0.5
    fconst[:, FC_INV1024] = 1.0 / 1024.0
    fconst[:, FC_HALF] = 0.5
    fconst[:, FC_ONE] = 1.0
    fconst[:, 5] = -1.0
    return dict(cq=tab(cos) * 0.125, sq=tab(sins) * 0.125, ck=tab(cos),
                sk=tab(sins),
                tri_lead=(qr >= kr).astype(np.float32),
                tri_trail=(qr < kr).astype(np.float32),
                perm=perm, ident=np.eye(P, dtype=np.float32), fconst=fconst,
                ones_row=np.ones((1, P), np.float32),
                ones_col=np.ones((P, 1), np.float32))


class Sem:
    def __init__(self, nc, es, name):
        self.h = es.enter_context(nc.semaphore(name))
        self.n = 0
        self.name = name


class Rec:
    ENGINES = ("sync", "scalar", "vector", "tensor", "gpsimd")

    def __init__(self, nc):
        self.nc = nc
        self.items = {e: [] for e in self.ENGINES}
        self.waited = {e: {} for e in self.ENGINES}

    def set_engine_sems(self, esems):
        self.esems = esems

    def op(self, engine, emit, waits=(), inc=None, inck=1, is_dma=False):
        flat = []
        for w in waits:
            if w is None:
                continue
            if isinstance(w, list):
                flat.extend(x for x in w if x is not None)
            else:
                flat.append(w)
        waits = tuple(flat)
        esem = getattr(self, "esems", {}).get(engine)
        if esem is not None and not is_dma:
            if esem.n > 0:
                waits = waits + ((esem, esem.n),)
            inc, inck = esem, 1
        ev = None
        if inc is not None:
            inc.n += inck
            ev = (inc, inc.n)
        self.items[engine].append((waits, emit, ev, inck))
        return ev

    def set_dma_sems(self, qsems):
        self.qsems = qsems
        self.qord = {e: 0 for e in qsems}

    def dma(self, engine, out, in_, waits=(), inc=None, **kw):
        sems = self.qsems[engine]
        k = self.qord[engine]
        self.qord[engine] = k + 1
        par = k % 2
        sem = sems[par]
        chain = (sem, 16 * (k // 2)) if k >= 2 else None

        def emit(eng, o=out, i=in_, kw=kw):
            return eng.dma_start(out=o, in_=i, **kw)
        return self.op(engine, emit, tuple(waits) + (chain,), sem, 16, is_dma=True)

    def flush(self):
        use = [e for e in self.ENGINES if self.items[e]]
        if not use:
            return
        with self.nc.Block() as blk:
            for e in use:
                items = self.items[e]
                waited = self.waited[e]

                def runner(eng, items=items, waited=waited):
                    for (waits, emit, ev, inck) in items:
                        for (s, c) in waits:
                            if waited.get(id(s), -1) < c:
                                eng.wait_ge(s.h, c)
                                waited[id(s)] = c
                        ins = emit(eng)
                        if ev is not None:
                            ins.then_inc(ev[0].h, inck)
                getattr(blk, e)(runner)
        for e in use:
            self.items[e] = []


def build_nc(n_blocks=NBLK, n_sub=3, name="mk", dbg=None, serial=False, lvl3=False, skew=2, serecip=False):
    nc = bass.Bass("TRN2", target_bir_lowering=False, name=f"{name}_{n_blocks}_{n_sub}")
    C = _consts()

    x_in = nc.dram_tensor("x_in", [NBLK * TB, D], F32, kind="ExternalInput").ap()
    wqkv = nc.dram_tensor("wqkv", [D, 3 * D], F32, kind="ExternalInput").ap()
    wout = nc.dram_tensor("wout", [D, D], F32, kind="ExternalInput").ap()
    gate_w = nc.dram_tensor("gate_w", [2, D], F32, kind="ExternalInput").ap()
    w1 = nc.dram_tensor("w1", [2, D, HID], F32, kind="ExternalInput").ap()
    b1 = nc.dram_tensor("b1", [2, HID], F32, kind="ExternalInput").ap()
    w2 = nc.dram_tensor("w2", [2, HID, D], F32, kind="ExternalInput").ap()
    b2 = nc.dram_tensor("b2", [2, D], F32, kind="ExternalInput").ap()
    sw1 = nc.dram_tensor("sw1", [D, HID], F32, kind="ExternalInput").ap()
    sb1 = nc.dram_tensor("sb1", [HID], F32, kind="ExternalInput").ap()
    sw2 = nc.dram_tensor("sw2", [HID, D], F32, kind="ExternalInput").ap()
    sb2 = nc.dram_tensor("sb2", [D], F32, kind="ExternalInput").ap()
    y = nc.dram_tensor("y", [NBLK * TB, D], F32, kind="ExternalOutput").ap()
    ydbg = nc.dram_tensor("ydbg", [P, 16384], F32, kind="ExternalOutput").ap() if dbg else None

    s_w1 = nc.dram_tensor("s_w1", [NETS * D, HID], BF16, kind="Internal").ap()
    s_w2 = nc.dram_tensor("s_w2", [NETS * HID, D], BF16, kind="Internal").ap()
    s_qkv = nc.dram_tensor("s_qkv", [D, 3 * D], BF16, kind="Internal").ap()
    s_wout = nc.dram_tensor("s_wout", [D, D], BF16, kind="Internal").ap()

    def bft(a):
        return np.ascontiguousarray(a.astype(mybir.dt.np(BF16)))
    c_cq = nc.inline_tensor(C["cq"], name="c_cq").ap()
    c_sq = nc.inline_tensor(C["sq"], name="c_sq").ap()
    c_ck = nc.inline_tensor(C["ck"], name="c_ck").ap()
    c_sk = nc.inline_tensor(C["sk"], name="c_sk").ap()
    c_lead = nc.inline_tensor(bft(C["tri_lead"]), name="c_lead").ap()
    c_trail = nc.inline_tensor(bft(C["tri_trail"]), name="c_trail").ap()
    c_perm = nc.inline_tensor(bft(C["perm"]), name="c_perm").ap()
    c_id_b = nc.inline_tensor(bft(C["ident"]), name="c_id_b").ap()
    c_fc = nc.inline_tensor(C["fconst"], name="c_fc").ap()
    c_or_b = nc.inline_tensor(bft(C["ones_row"]), name="c_or_b").ap()
    c_or_f = nc.inline_tensor(C["ones_row"], name="c_or_f").ap()
    c_oc_b = nc.inline_tensor(bft(C["ones_col"]), name="c_oc_b").ap()
    c_z16 = nc.inline_tensor(np.zeros((1, 16), np.float32), name="c_z16").ap()

    with ExitStack() as es:
        ee = es.enter_context
        LDq = [Sem(nc, es, "LDa"), Sem(nc, es, "LDb")]
        SCq = [Sem(nc, es, "SCa"), Sem(nc, es, "SCb")]
        GDq = [Sem(nc, es, "GDa"), Sem(nc, es, "GDb")]
        PE = Sem(nc, es, "PE")
        SE = Sem(nc, es, "SE")
        VE = Sem(nc, es, "VE")
        PO = Sem(nc, es, "PO")

        x_tm = ee(nc.sbuf_tensor("x_tm", [P, NTC * D], F32))
        kc_sb = ee(nc.sbuf_tensor("kc_sb", [P, NHC * 1024], BF16))
        vc_sb = ee(nc.sbuf_tensor("vc_sb", [P, 8 * 1040], BF16))
        xln_fm = ee(nc.sbuf_tensor("xln_fm", [P, NTC * D], BF16))
        idb = ee(nc.sbuf_tensor("idb", [P, P], BF16))
        permb = ee(nc.sbuf_tensor("permb", [P, P], BF16))
        or_b = ee(nc.sbuf_tensor("or_b", [1, P], BF16))
        or_f = ee(nc.sbuf_tensor("or_f", [1, P], F32))
        oc_b = ee(nc.sbuf_tensor("oc_b", [P, 1], BF16))
        lead_sb = ee(nc.sbuf_tensor("lead_sb", [P, P], BF16))
        trail_sb = ee(nc.sbuf_tensor("trail_sb", [P, P], BF16))
        fc = ee(nc.sbuf_tensor("fc", [P, 8], F32))
        b1s = ee(nc.sbuf_tensor("b1s", [P, NETS * 32], F32))
        gw = ee(nc.sbuf_tensor("gw", [P, NFC * 2], BF16))
        b2b = ee(nc.sbuf_tensor("b2b", [P, D], F32))
        stats = ee(nc.sbuf_tensor("stats", [P, 28], F32))
        negm128 = ee(nc.sbuf_tensor("negm128", [P, 1], F32))
        negmH = ee(nc.sbuf_tensor("negmH", [P, NH], F32))
        xlt = ee(nc.sbuf_tensor("xlt", [P, 2 * D], BF16))
        xsq = ee(nc.sbuf_tensor("xsq", [P, D], F32))
        rows = ee(nc.sbuf_tensor("rows", [1, 4096], F32))
        rows2 = ee(nc.sbuf_tensor("rows2", [1, 128], F32))
        rows_b = ee(nc.sbuf_tensor("rows_b", [1, 16], BF16))
        m0row = ee(nc.sbuf_tensor("m0row", [1, TB], F32))
        g1row = ee(nc.sbuf_tensor("g1row", [1, TB], F32))
        cqs = ee(nc.sbuf_tensor("cqs", [P, TB], F32))
        sqt = ee(nc.sbuf_tensor("sqt", [P, TB], F32))
        cks = ee(nc.sbuf_tensor("cks", [P, TB], F32))
        sks = ee(nc.sbuf_tensor("sks", [P, TB], F32))

        fcell = ee(nc.sbuf_tensor("fcell", [1, 16], F32))

        R = Rec(nc)
        R.set_dma_sems({"sync": LDq, "scalar": SCq, "gpsimd": GDq})
        R.set_engine_sems({"vector": VE, "scalar": SE, "gpsimd": PO})
        _uid = [0]

        e_fcell = [None]

        def fence(engine, inc, waits=()):
            waits = tuple(waits) + (e_fcell[0],)
            """Tiny datapath op on `engine` carrying `inc`: its completion
            implies all prior same-engine ops' writebacks are visible
            (engine issue waits for pipe-empty). Use for cross-engine RAW
            edges within a Block."""
            if engine == "vector":
                emit = lambda v: v.tensor_copy(out=fcell[0:1, 0:1], in_=fcell[0:1, 1:2])
            elif engine == "scalar":
                emit = lambda sc: sc.activation(out=fcell[0:1, 2:3], in_=fcell[0:1, 3:4], func=AF.Copy)
            elif engine == "gpsimd":
                emit = lambda g: g.tensor_copy(out=fcell[0:1, 4:5], in_=fcell[0:1, 5:6])
            else:
                raise ValueError(engine)
            return R.op(engine, emit, waits, inc)

        def un(base):
            _uid[0] += 1
            return f"{base}_{_uid[0]}"

        def rsc(i):
            return rows2[0:1, i:i + 1]

        e_fcell[0] = R.dma("scalar", fcell[:], c_z16)
        e_sw1 = R.dma("gpsimd", s_w1[0:D, :], sw1)
        e_w1 = R.dma("gpsimd", s_w1[D:3 * D, :], w1.rearrange("e k h -> (e k) h"))
        e_sw2 = R.dma("gpsimd", s_w2[0:HID, :], sw2)
        e_w2c = R.dma("gpsimd", s_w2[HID:3 * HID, :], w2.rearrange("e k h -> (e k) h"))
        e_qkvc = R.dma("gpsimd", s_qkv, wqkv)
        e_woutc = R.dma("gpsimd", s_wout, wout)

        ev_c = []
        for dst, src in ((idb, c_id_b), (permb, c_perm), (or_b, c_or_b),
                         (or_f, c_or_f), (oc_b, c_oc_b), (lead_sb, c_lead),
                         (trail_sb, c_trail), (fc, c_fc)):
            ev_c.append(R.dma("scalar", dst[:], src))

        with ExitStack() as ps:
            trp = ps.enter_context(nc.psum_tensor("p_tr0", [P, 512], F32))
            srcs = [sb1[None, :], b1[0:1, :], b1[1:2, :]]
            evmm = None
            for net in range(NETS):
                e_rd = R.dma("scalar", rows[0:1, :], srcs[net],
                             waits=[evmm])
                for hcb in range(32):
                    src = rows[0:1, hcb * P:(hcb + 1) * P]
                    col = net * 32 + hcb
                    evmm = R.op("tensor", (lambda t, s=src, c=col: t.matmul(
                        out=trp[:, c:c + 1], lhsT=s, rhs=fc[0:1, FC_ONE:FC_ONE + 1],
                        start=True, stop=True)),
                        waits=[e_rd, ev_c[-1]], inc=PE)
            e_b1s = R.op("vector", lambda v: v.tensor_copy(out=b1s[:], in_=trp[:, 0:96]),
                         waits=[evmm], inc=VE)
            e_gr0 = R.dma("scalar", rows[0:1, 0:D], gate_w[0:1, :], waits=[evmm])
            e_gr1 = R.dma("scalar", rows[0:1, D:2 * D], gate_w[1:2, :])
            evg = None
            for kcb in range(NFC):
                for eexp in range(2):
                    src = rows[0:1, eexp * D + kcb * P: eexp * D + (kcb + 1) * P]
                    col = 96 + kcb * 2 + eexp
                    evg = R.op("tensor", (lambda t, s=src, c=col: t.matmul(
                        out=trp[:, c:c + 1], lhsT=s, rhs=fc[0:1, FC_ONE:FC_ONE + 1],
                        start=True, stop=True)), waits=[e_gr0, e_gr1], inc=PE)
            e_gw = R.op("vector", lambda v: v.tensor_copy(out=gw[:], in_=trp[:, 96:112]),
                        waits=[evg], inc=VE)
            e_b2a = R.dma("scalar", rows[0:1, 0:D], b2[0:1, :], waits=[evg])
            e_b2c = R.dma("scalar", rows[0:1, D:2 * D], b2[1:2, :])
            e_sb2 = R.dma("scalar", rows[0:1, 2 * D:3 * D], sb2[None, :])
            e_s1 = R.op("vector", lambda v: v.tensor_add(
                out=rows[0:1, 0:D], in0=rows[0:1, 0:D],
                in1=rows[0:1, D:2 * D]), waits=[e_b2a, e_b2c, e_b1s, e_gw], inc=VE)
            e_s2 = R.op("vector", lambda v: v.tensor_add(
                out=rows[0:1, 0:D], in0=rows[0:1, 0:D],
                in1=rows[0:1, 2 * D:3 * D]), waits=[e_sb2], inc=VE)
            ebb = []
            for hh in range(2):
                evb = R.op("tensor", (lambda t, hh=hh: t.matmul(
                    out=trp[:, 0:512], lhsT=or_f[0:1, :],
                    rhs=rows[0:1, hh * 512:(hh + 1) * 512],
                    start=True, stop=True)),
                    waits=[e_s2, (ebb[-1] if ebb else None)], inc=PE)
                ebb.append(R.op("vector", (lambda v, hh=hh: v.tensor_copy(
                    out=b2b[:, hh * 512:(hh + 1) * 512], in_=trp[:, 0:512])),
                    waits=[evb], inc=VE))
            R.flush()

        e_const = ev_c[-1]
        ev_prolog_b2b = ebb[-1]
        st = {"x_ready": None, "x_free": None, "xln_ready": None,
              "xln_free": None, "kmax_valid": False, "lvl3": lvl3,
              "skew": skew, "serecip": serecip}

        def ln_phase():
            with ExitStack() as ps:
                trp = ps.enter_context(nc.psum_tensor(un("p_trl"), [P, D], BF16))
                e_ssq = None
                xw = [st["x_ready"], e_const]
                e_sum = R.op("vector", lambda v: v.tensor_reduce(
                    out=stats[:, 0:4],
                    in_=x_tm[:].rearrange("p (tc d) -> p tc d", tc=NTC),
                    axis=AX.X, op=ALU.add), waits=xw, inc=VE)
                for tc in range(NTC):
                    xs = x_tm[:, tc * D:(tc + 1) * D]
                    e_ssq = R.op("scalar", (lambda s, xs=xs, tc=tc: s.activation(
                        out=xsq[:], in_=xs, func=AF.Square,
                        accum_out=stats[:, 4 + tc:5 + tc])), waits=xw, inc=SE)
                e_m = R.op("vector", lambda v: v.tensor_scalar(
                    out=stats[:, 8:12], in0=stats[:, 0:4], scalar1=1.0 / 1024.0,
                    scalar2=None, op0=ALU.mult), waits=[e_ssq, e_sum], inc=VE)
                e_t = R.op("vector", lambda v: v.tensor_tensor(
                    out=stats[:, 12:16], in0=stats[:, 8:12], in1=stats[:, 8:12],
                    op=ALU.mult), inc=VE)
                e_var = R.op("vector", lambda v: v.scalar_tensor_tensor(
                    out=stats[:, 16:20], in0=stats[:, 4:8], scalar=1.0 / 1024.0,
                    in1=stats[:, 12:16], op0=ALU.mult, op1=ALU.subtract),
                    waits=[e_t], inc=VE)
                e_ln = R.op("scalar", lambda s: s.activation(
                    out=stats[:, 20:24], in_=stats[:, 16:20], func=AF.Ln,
                    bias=fc[:, FC_EPS:FC_EPS + 1]), waits=[e_var], inc=SE)
                R.op("scalar", lambda s: s.activation(
                    out=stats[:, 24:28], in_=stats[:, 20:24], func=AF.Exp,
                    scale=fc[:, FC_NHALF:FC_NHALF + 1]), waits=[e_ln], inc=None)
                e_rstd = fence("scalar", SE)
                slot_free = [None, None]
                e_xf = None
                for tc in range(NTC):
                    slot = tc % 2
                    xs = x_tm[:, tc * D:(tc + 1) * D]
                    dst = xlt[:, slot * D:(slot + 1) * D]
                    wl = [e_rstd, e_m, slot_free[slot]]
                    R.op("vector", (lambda v, xs=xs, dst=dst, tc=tc: v.tensor_scalar(
                        out=dst, in0=xs, scalar1=stats[:, 8 + tc:9 + tc],
                        scalar2=stats[:, 24 + tc:25 + tc],
                        op0=ALU.subtract, op1=ALU.mult)), waits=wl, inc=None)
                    e_app = fence("vector", VE)
                    e_tr = None
                    wl2 = [e_app, e_xf, st["xln_free"] if tc == 0 else None]
                    for fcb in range(NFC):
                        src = xlt[:, slot * D + fcb * P: slot * D + (fcb + 1) * P]
                        e_tr = R.op("tensor", (lambda t, src=src, fcb=fcb: t.transpose(
                            out=trp[:, fcb * P:(fcb + 1) * P], in_=src,
                            identity=idb[:])), waits=wl2, inc=PE)
                        wl2 = []
                    e_xf = R.op("vector", (lambda v, tc=tc: v.tensor_copy(
                        out=xln_fm[:, tc * D:(tc + 1) * D], in_=trp[:])),
                        waits=[e_tr], inc=VE)
                    slot_free[slot] = e_xf
                st["xln_ready"] = e_xf
                st["xln_free"] = None
                if dbg == "ln" and not st.get("dbg_done"):
                    st["dbg_done"] = True
                    R.dma("gpsimd", ydbg[:, 0:28], stats[:], waits=[e_xf])
                    R.dma("gpsimd", ydbg[:, 1024:1024 + 4096], xln_fm[:], waits=[e_xf])
                    R.dma("gpsimd", ydbg[:, 6144:6144 + 1024], x_tm[:, 0:1024], waits=[e_xf])
                R.flush()

        def ffn_phase():
            gate = st["x_ready"]
            ln_phase()
            xv = xln_fm[:].rearrange("p (tc fcb o) -> p tc fcb o", tc=NTC, fcb=NFC)
            with ExitStack() as ps:
                pse = ps.enter_context
                m0_fm = pse(nc.sbuf_tensor(un("m0_fm"), [P, NTC * D], BF16))
                mbb = pse(nc.sbuf_tensor(un("mbb"), [P, TB], BF16))
                h_sb = [pse(nc.sbuf_tensor(un(f"h{i}"), [P, NHK * TB], BF16)) for i in range(2)]
                w1b = [pse(nc.sbuf_tensor(un(f"w1b{i}"), [P, NFC * HB], BF16)) for i in range(2)]
                w2b = [pse(nc.sbuf_tensor(un(f"w2b{i}"), [P, NHK * D], BF16)) for i in range(2)]
                acc = pse(nc.sbuf_tensor(un("acc"), [P, NTC * D], F32))
                zp = [pse(nc.psum_tensor(un(f"zp{i}"), [P, TB], F32)) for i in range(3)]
                op = [pse(nc.psum_tensor(un(f"op{i}"), [P, TB], F32)) for i in range(3)]
                gp = pse(nc.psum_tensor(un("gp"), [1, TB], F32))
                mbp = pse(nc.psum_tensor(un("mbp"), [P, TB], F32))

                m0v = m0_fm[:].rearrange("p (tc fcb o) -> p tc fcb o", tc=NTC, fcb=NFC)
                mbv = mbb[:].rearrange("p (tc o) -> p tc o", tc=NTC)
                xln_ev = st["xln_ready"]

                e_ge = [None, None]
                for eexp in range(2):
                    evm = None
                    wl = [xln_ev, e_gw] if eexp == 0 else [e_g0c]
                    for kcb in range(NFC):
                        lhs = gw[:, kcb * 2 + eexp: kcb * 2 + eexp + 1]
                        evm = R.op("tensor", (lambda t, lhs=lhs, kcb=kcb: t.matmul(
                            out=gp[0:1, 0:TB], lhsT=lhs,
                            rhs=xv[:, :, kcb, :], start=(kcb == 0),
                            stop=(kcb == NFC - 1))), waits=wl, inc=PE)
                        wl = []
                    e_ge[eexp] = evm
                    if eexp == 0:
                        R.op("vector", lambda v: v.tensor_copy(
                            out=g1row[:], in_=gp[0:1, 0:TB]), waits=[evm], inc=None)
                        e_g0c = fence("vector", VE)
                R.op("vector", lambda v: v.tensor_tensor(
                    out=m0row[:], in0=gp[0:1, 0:TB], in1=g1row[:], op=ALU.is_le),
                    waits=[e_ge[1]], inc=None)
                e_m0r = fence("vector", VE)
                e_mbp = R.op("tensor", lambda t: t.matmul(
                    out=mbp[:], lhsT=or_f[0:1, :], rhs=m0row[:], start=True,
                    stop=True), waits=[e_m0r], inc=PE)
                e_mbb = R.op("vector", lambda v: v.tensor_copy(out=mbb[:], in_=mbp[:]),
                             waits=[e_mbp], inc=VE)
                for fcb in range(NFC):
                    R.op("vector", (lambda v, fcb=fcb: v.tensor_tensor(
                        out=m0v[:, :, fcb, :], in0=xv[:, :, fcb, :], in1=mbv,
                        op=ALU.mult)), waits=[e_mbb], inc=None)
                e_m0 = fence("vector", VE)

                rounds = [(net, hb) for net in range(NETS) for hb in range(NHBR)]
                e_w1d = [None] * (NROUND + 2)
                e_w2d = [None] * (NROUND + 2)
                e_zg = {}
                e_relu = {}
                e_outg = {}
                e_add = {}
                e_m1 = None

                def w1_dma(r, waits):
                    net, hb = rounds[r]
                    return R.dma("sync", w1b[r % 2][:].rearrange(
                        "p (kcb h) -> p kcb h", kcb=NFC),
                        s_w1[net * D:(net + 1) * D, hb * HB:(hb + 1) * HB].rearrange(
                            "(kcb p) h -> p kcb h", p=P),
                        waits=waits + [e_sw1, e_w1])

                def w2_dma(r, waits):
                    net, hb = rounds[r]
                    return R.dma("sync", w2b[r % 2][:].rearrange(
                        "p (hk o) -> p hk o", hk=NHK),
                        s_w2[net * HID + hb * HB: net * HID + (hb + 1) * HB, :].rearrange(
                            "(hk p) o -> p hk o", p=P),
                        waits=waits + [e_sw2, e_w2c])

                for r in range(min(2, NROUND)):
                    e_w1d[r] = w1_dma(r, [gate])
                    e_w2d[r] = w2_dma(r, [gate])

                gg = 0

                def emit_z(r):
                    net, hb = rounds[r]
                    if net == 0:
                        rhs, rw = xv, [xln_ev]
                    elif net == 1:
                        rhs, rw = m0v, [e_m0]
                    else:
                        rhs, rw = xv, [st.get("m1ev")]
                    for hc in range(NHK):
                        wl = [e_w1d[r]] + rw
                        if r > 0 and hc == 0 and st.get("lvl3", True):
                            wl.append(st.get("relu_f_prev"))
                        rw = []
                        zg = r * NHK + hc
                        zgw = r * NHK + hc - 2
                        if zgw >= 0:
                            wl.append(e_relu.get((zgw // NHK, zgw % NHK)))
                        evz = None
                        for kcb in range(NFC):
                            lhs = w1b[r % 2][:, kcb * HB + hc * P: kcb * HB + (hc + 1) * P]
                            evz = R.op("tensor", (lambda t, lhs=lhs, rhs=rhs, kcb=kcb, zg=zg: t.matmul(
                                out=zp[zg % 3][:], lhsT=lhs, rhs=rhs[:, :, kcb, :],
                                start=(kcb == 0), stop=(kcb == NFC - 1))),
                                waits=wl, inc=PE)
                            wl = []
                        e_zg[(r, hc)] = evz
                        bcol = net * 32 + hb * NHK + hc
                        wl = [evz, e_b1s]
                        if r >= 2 and hc == 0:
                            wl.append(e_zg.get((r - 1, 0)))
                        e_relu[(r, hc)] = R.op("scalar", (lambda s, hc=hc, r=r, bcol=bcol, zg=zg: s.activation(
                            out=h_sb[r % 2][:, hc * TB:(hc + 1) * TB],
                            in_=zp[zg % 3][:], func=AF.Relu,
                            bias=b1s[:, bcol:bcol + 1])), waits=wl, inc=SE)
                    if net == 0 and hb == NHBR - 1:
                        for fcb in range(NFC):
                            R.op("vector", (lambda v, fcb=fcb: v.tensor_tensor(
                                out=xv[:, :, fcb, :], in0=xv[:, :, fcb, :],
                                in1=m0v[:, :, fcb, :], op=ALU.subtract)),
                                waits=[e_zg[(r, NHK - 1)], e_m0], inc=None)
                        st["m1ev"] = fence("vector", VE)
                    if r + 2 < NROUND:
                        e_w1d[r + 2] = w1_dma(r + 2, [e_zg[(r, NHK - 1)]])
                    st["relu_f_prev"] = fence("scalar", SE)

                def emit_out(r):
                    nonlocal gg
                    gi = 0
                    for tc in range(NTC):
                        for half in range(2):
                            wl = [relu_fr[r], e_w2d[r], e_add.get(gg - 2)]
                            evo = None
                            for hk in range(NHK):
                                lhs = h_sb[r % 2][:, hk * TB + tc * P: hk * TB + (tc + 1) * P]
                                rhsw = w2b[r % 2][:, hk * D + half * 512: hk * D + (half + 1) * 512]
                                evo = R.op("tensor", (lambda t, lhs=lhs, rhsw=rhsw, hk=hk, g=gg: t.matmul(
                                    out=op[g % 3][:], lhsT=lhs, rhs=rhsw,
                                    start=(hk == 0), stop=(hk == NHK - 1))),
                                    waits=wl, inc=PE)
                                wl = []
                            e_outg[(r, gi)] = evo
                            accs = acc[:, tc * D + half * 512: tc * D + (half + 1) * 512]
                            src1 = b2b[:, half * 512:(half + 1) * 512] if r == 0 else accs
                            e_add[gg] = R.op("vector", (lambda v, accs=accs, src1=src1, g=gg: v.tensor_tensor(
                                out=accs, in0=op[g % 3][:], in1=src1, op=ALU.add)),
                                waits=[evo, ev_prolog_b2b, e_add.get(gg - 8)], inc=VE)
                            gg += 1
                            gi += 1
                    if r + 2 < NROUND:
                        e_w2d[r + 2] = w2_dma(r + 2, [e_outg[(r, NTC * 2 - 1)]])

                relu_fr = {}
                for r in range(NROUND):
                    emit_z(r)
                    relu_fr[r] = st["relu_f_prev"]
                    if r >= 1:
                        emit_out(r - 1)
                emit_out(NROUND - 1)

                e_epi = None
                last_add = e_add[gg - 1]
                for tc in range(NTC):
                    xs = x_tm[:, tc * D:(tc + 1) * D]
                    e_h = R.op("vector", (lambda v, tc=tc: v.tensor_scalar(
                        out=acc[:, tc * D:(tc + 1) * D],
                        in0=acc[:, tc * D:(tc + 1) * D], scalar1=0.5,
                        scalar2=None, op0=ALU.mult)),
                        waits=[last_add], inc=VE)
                    e_epi = R.op("vector", (lambda v, xs=xs, tc=tc: v.tensor_tensor(
                        out=xs, in0=acc[:, tc * D:(tc + 1) * D], in1=xs,
                        op=ALU.add)), waits=[e_h, gate], inc=VE)
                st["x_ready"] = e_epi
                st["xln_free"] = e_zg[(NROUND - 1, NHK - 1)]
                R.flush()

        def attn_phase(b):
            gate = st["x_ready"]
            brel = b % 4
            pos0 = brel * TB
            curk = pos0 % 1024
            ln_phase()
            xv = xln_fm[:].rearrange("p (tc fcb o) -> p tc fcb o", tc=NTC, fcb=NFC)
            xln_ev = st["xln_ready"]
            e_cs = []
            for dst, src in ((cqs, c_cq), (sqt, c_sq), (cks, c_ck), (sks, c_sk)):
                e_cs.append(R.dma("scalar", dst[:], src[:, pos0:pos0 + TB]))

            with ExitStack() as outer:
                q_fm = outer.enter_context(nc.sbuf_tensor(un("q_fm"), [P, NHC * TB], BF16))
                af = outer.enter_context(nc.sbuf_tensor(un("af"), [P, NHC * TB], BF16))
                qtmp = outer.enter_context(nc.sbuf_tensor(un("qtmp"), [P, TB], BF16))
                sqtmp = outer.enter_context(nc.sbuf_tensor(un("sqtmp"), [P, 2 * TB], BF16))
                woutb = outer.enter_context(nc.sbuf_tensor(un("woutb"), [P, NFC * D], BF16))

                with ExitStack() as ps:
                    pse = ps.enter_context
                    qkvw = [pse(nc.sbuf_tensor(un(f"qkvw{i}"), [P, NFC * 2 * P], BF16)) for i in range(2)]
                    wv_sb = pse(nc.sbuf_tensor(un("wv_sb"), [P, NFC * D], BF16))
                    rt0 = pse(nc.sbuf_tensor(un("rt0"), [P, TB], F32))
                    rt1 = pse(nc.sbuf_tensor(un("rt1"), [P, TB], F32))
                    pa = pse(nc.psum_tensor(un("pa"), [P, TB], F32))
                    pb = pse(nc.psum_tensor(un("pb"), [P, TB], F32))
                    pc = pse(nc.psum_tensor(un("pc"), [P, TB], F32))
                    nr = [pse(nc.psum_tensor(un(f"nr{i}"), [1, TB], F32)) for i in range(3)]
                    nr3 = pse(nc.psum_tensor(un("nr3"), [P, 16], F32))

                    e_wv = R.dma("sync", wv_sb[:].rearrange("p (kcb o) -> p kcb o", kcb=NFC),
                                 s_qkv[:, 2 * D:3 * D].rearrange("(kcb p) o -> p kcb o", p=P),
                                 waits=[e_qkvc, gate])
                    sq4 = s_qkv.rearrange("(kcb p) (m o) -> p kcb m o", p=P, m=3)
                    e_qkd = [None] * NHC

                    def qk_dma(hc, waits):
                        dv = qkvw[hc % 2][:].rearrange(
                            "p (kcb m o) -> p kcb m o", kcb=NFC, m=2)
                        ev1 = R.dma("sync", dv[:, :, 0, :], sq4[:, :, 0, hc * P:(hc + 1) * P],
                                    waits=waits + [e_qkvc, gate])
                        ev2 = R.dma("sync", dv[:, :, 1, :],
                                    sq4[:, :, 1, hc * P:(hc + 1) * P])
                        return [ev1, ev2]
                    for hc in range(min(2, NHC)):
                        e_qkd[hc] = qk_dma(hc, [])

                    e_rope_q = [None] * NHC
                    e_rope_k = [None] * NHC
                    pa_free = None
                    pb_free = None
                    qn_free = [None, None]
                    qn_red = [None, None]
                    e_kn = None
                    for hc in range(NHC):
                        wv4 = qkvw[hc % 2][:].rearrange("p (kcb m o) -> p kcb m o", kcb=NFC, m=2)
                        evq = None
                        wl = [e_qkd[hc], xln_ev, pa_free]
                        for kcb in range(NFC):
                            evq = R.op("tensor", (lambda t, wv4=wv4, kcb=kcb: t.matmul(
                                out=pa[:], lhsT=wv4[:, kcb, 0, :], rhs=xv[:, :, kcb, :],
                                start=(kcb == 0), stop=(kcb == NFC - 1))), waits=wl, inc=PE)
                            wl = []
                        R.op("vector", lambda v: v.tensor_copy(
                            out=qtmp[:], in_=pa[:]), waits=[evq], inc=None)
                        e_qt = fence("vector", VE)
                        e_rot = R.op("tensor", lambda t: t.matmul(
                            out=pb[:], lhsT=permb[:], rhs=qtmp[:], start=True,
                            stop=True), waits=[e_qt, pb_free, e_const], inc=PE)
                        e_c0 = R.op("vector", lambda v: v.tensor_tensor(
                            out=rt0[:], in0=pa[:], in1=cqs[:], op=ALU.mult),
                            waits=[evq, e_cs[0]], inc=VE)
                        e_c1 = R.op("vector", lambda v: v.tensor_tensor(
                            out=rt1[:], in0=pb[:], in1=sqt[:], op=ALU.mult),
                            waits=[e_rot, e_cs[1]], inc=VE)
                        e_rope_q[hc] = R.op("vector", (lambda v, hc=hc: v.tensor_tensor(
                            out=q_fm[:, hc * TB:(hc + 1) * TB], in0=rt0[:],
                            in1=rt1[:], op=ALU.add)), inc=VE)
                        R.op("scalar", (lambda sc, hc=hc: sc.activation(
                            out=sqtmp[:, 0:TB], in_=q_fm[:, hc * TB:(hc + 1) * TB],
                            func=AF.Square)), waits=[e_rope_q[hc], qn_free[0], qn_free[1]], inc=None)
                        e_sqq = fence("scalar", SE)
                        for half in range(2):
                            hh = 2 * hc + half
                            evn = R.op("tensor", (lambda t, half=half: t.matmul(
                                out=nr[half][0:1, 0:TB],
                                lhsT=oc_b[half * 64:half * 64 + 64, 0:1],
                                rhs=sqtmp[half * 64:half * 64 + 64, 0:TB],
                                start=True, stop=True)),
                                waits=[e_sqq, qn_red[half]], inc=PE)
                            R.op("vector", (lambda v, hh=hh, half=half: v.tensor_reduce(
                                out=rows2[0:1, 16 + hh:17 + hh], in_=nr[half][0:1, 0:TB],
                                axis=AX.X, op=ALU.max)), waits=[evn], inc=None)
                            qn_red[half] = fence("vector", VE)
                        qn_free[0] = evn
                        evk = None
                        wl = [e_c0]
                        for kcb in range(NFC):
                            evk = R.op("tensor", (lambda t, wv4=wv4, kcb=kcb: t.matmul(
                                out=pa[:], lhsT=wv4[:, kcb, 1, :], rhs=xv[:, :, kcb, :],
                                start=(kcb == 0), stop=(kcb == NFC - 1))), waits=wl, inc=PE)
                            wl = []
                        R.op("vector", lambda v: v.tensor_copy(
                            out=qtmp[:], in_=pa[:]), waits=[evk, e_rope_q[hc]], inc=None)
                        e_kt = fence("vector", VE)
                        e_rotk = R.op("tensor", lambda t: t.matmul(
                            out=pb[:], lhsT=permb[:], rhs=qtmp[:], start=True,
                            stop=True), waits=[e_kt, e_c1], inc=PE)
                        e_c0k = R.op("vector", lambda v: v.tensor_tensor(
                            out=rt0[:], in0=pa[:], in1=cks[:], op=ALU.mult),
                            waits=[evk, e_cs[2], e_rope_q[hc]], inc=VE)
                        e_c1k = R.op("vector", lambda v: v.tensor_tensor(
                            out=rt1[:], in0=pb[:], in1=sks[:], op=ALU.mult),
                            waits=[e_rotk, e_cs[3]], inc=VE)
                        e_rope_k[hc] = R.op("vector", (lambda v, hc=hc: v.tensor_tensor(
                            out=kc_sb[:, hc * 1024 + curk: hc * 1024 + curk + TB],
                            in0=rt0[:], in1=rt1[:], op=ALU.add)), inc=VE)
                        R.op("scalar", (lambda sc, hc=hc: sc.activation(
                            out=sqtmp[:, TB:2 * TB],
                            in_=kc_sb[:, hc * 1024 + curk: hc * 1024 + curk + TB],
                            func=AF.Square)), waits=[e_rope_k[hc], qn_free[1]], inc=None)
                        e_sqk = fence("scalar", SE)
                        e_kn = R.op("tensor", (lambda t, hc=hc: t.matmul(
                            out=nr[2][0:1, 0:TB], lhsT=oc_b[:], rhs=sqtmp[:, TB:2 * TB],
                            start=(hc == 0), stop=(hc == NHC - 1))),
                            waits=[e_sqk], inc=PE)
                        qn_free[1] = e_kn
                        pa_free = e_c1k
                        pb_free = e_rope_k[hc]
                        if hc + 2 < NHC:
                            e_qkd[hc + 2] = qk_dma(hc + 2, [evk])

                    e_vlast_ve = None
                    e_vlast_pe = None
                    for tc in range(NTC):
                        vslot = ((pos0 // P) + tc) % 8
                        R.op("gpsimd", (lambda g, vslot=vslot: g.memset(
                            vc_sb[:, vslot * 1040:(vslot + 1) * 1040].rearrange(
                                "p (h c) -> p h c", h=NH)[:, :, 64:65], 1.0)),
                            waits=[st.get("vslot_free")])
                        for half in range(2):
                            evv = None
                            wl = [e_wv, xln_ev, e_vlast_ve]
                            for kcb in range(NFC):
                                lhs = xv[:, tc, kcb, :]
                                rhsw = wv_sb[:, kcb * D + half * 512: kcb * D + (half + 1) * 512]
                                evv = R.op("tensor", (lambda t, lhs=lhs, rhsw=rhsw, kcb=kcb: t.matmul(
                                    out=pc[:], lhsT=lhs, rhs=rhsw,
                                    start=(kcb == 0), stop=(kcb == NFC - 1))),
                                    waits=wl, inc=PE)
                                wl = []
                            e_vlast_pe = evv
                            dstv = vc_sb[:, vslot * 1040:(vslot + 1) * 1040].rearrange(
                                "p (h c) -> p h c", h=NH)[:, half * 8:(half + 1) * 8, 0:64]
                            R.op("vector", (lambda v, dstv=dstv: v.tensor_copy(
                                out=dstv, in_=pc[:].rearrange("p (h c) -> p h c", h=8))),
                                waits=[evv], inc=None)
                            e_vlast_ve = fence("vector", VE)

                    e_km = R.op("vector", lambda v: v.tensor_reduce(
                        out=rsc(1), in_=nr[2][0:1, 0:TB], axis=AX.X, op=ALU.max),
                        waits=[e_kn, qn_red[0], qn_red[1]], inc=VE)
                    if brel > 0 and st["kmax_valid"]:
                        e_km = R.op("vector", lambda v: v.tensor_tensor(
                            out=rsc(1), in0=rsc(1), in1=rsc(2), op=ALU.max), inc=VE)
                    R.op("vector", lambda v: v.tensor_copy(out=rsc(2), in_=rsc(1)), inc=None)
                    e_kst = fence("vector", VE)
                    st["kmax_valid"] = True
                    e_l1 = R.op("scalar", lambda sc: sc.activation(
                        out=rows2[0:1, 32:48], in_=rows2[0:1, 16:32], func=AF.Ln),
                        waits=[e_kst], inc=SE)
                    R.op("scalar", lambda sc: sc.activation(
                        out=rsc(4), in_=rsc(1), func=AF.Ln), waits=[e_kst], inc=None)
                    e_l2 = fence("scalar", SE)
                    R.op("vector", lambda v: v.tensor_scalar(
                        out=rows2[0:1, 48:64], in0=rows2[0:1, 32:48],
                        scalar1=rsc(4), scalar2=None, op0=ALU.add),
                        waits=[e_l1, e_l2], inc=None)
                    e_ls = fence("vector", VE)
                    R.op("scalar", lambda sc: sc.activation(
                        out=rows2[0:1, 32:48], in_=rows2[0:1, 48:64], func=AF.Exp,
                        scale=fc[0:1, FC_HALF:FC_HALF + 1]), waits=[e_ls], inc=None)
                    e_M = fence("scalar", SE)
                    R.op("vector", lambda v: v.tensor_scalar(
                        out=rows2[0:1, 48:64], in0=rows2[0:1, 32:48], scalar1=-1.0,
                        scalar2=None, op0=ALU.mult), waits=[e_M], inc=None)
                    e_nM = fence("vector", VE)
                    e_nMb = R.op("tensor", lambda t: t.matmul(
                        out=nr3[:, 0:16], lhsT=or_f[0:1, :], rhs=rows2[0:1, 48:64],
                        start=True, stop=True), waits=[e_nM], inc=PE)
                    e_negm = R.op("vector", lambda v: v.tensor_copy(
                        out=negmH[:], in_=nr3[:, 0:16]), waits=[e_nMb], inc=VE)
                    st["pa_free"] = pa_free
                    st["pb_free"] = pb_free
                    st["xln_free"] = e_vlast_pe
                    R.flush()

                chunks = []
                for ac in range(max(0, pos0 // P - 4), pos0 // P + 4):
                    K0 = ac * P
                    ql = max(0, K0 - pos0)
                    qh = min(TB, K0 + 640 - pos0)
                    lead = (K0 >= pos0)
                    chunks.append(dict(kcol=K0 % 1024, ql=ql, qh=qh, lead=lead,
                                       mq0=(ql if lead else qh - P),
                                       vslot=ac % 8))
                with ExitStack() as ps:
                    pse = ps.enter_context
                    probs = pse(nc.sbuf_tensor(un("probs"), [P, 6 * TB], BF16))
                    recip_sb = pse(nc.sbuf_tensor(un("recip_sb"), [1, 2 * TB], F32))
                    bcast_sb = pse(nc.sbuf_tensor(un("bcast_sb"), [P, TB], F32))
                    sbank = [pse(nc.psum_tensor(un(f"sbank{i}"), [P, TB], F32)) for i in range(4)]
                    outp = [pse(nc.psum_tensor(un(f"outp{i}"), [P, TB], F32)) for i in range(2)]
                    bcp = pse(nc.psum_tensor(un("bcp"), [P, TB], F32))

                    e_wd = R.dma("sync", woutb[:].rearrange("p (kcb o) -> p kcb o", kcb=NFC),
                                 s_wout.rearrange("(kcb p) o -> p kcb o", p=P),
                                 waits=[e_woutc, st.get("woutb_free")])
                    slot_free = [None] * 6
                    bank_free = {0: st["pa_free"], 1: st["pb_free"], 2: e_vlast_ve, 3: e_negm}
                    head_free = [qn_red[0], qn_red[1]]
                    bcp_free = e_negm
                    cc = 0
                    e_pair_ev = [None] * NH
                    e_om_last = None

                    def emit_chunk(h, ci, ch):
                        nonlocal cc
                        hc = h // 2
                        hrow = (h % 2) * 64
                        bk = sbank[cc % 4]
                        N = ch["qh"] - ch["ql"]
                        slot = cc % 6
                        klhs = kc_sb[hrow:hrow + 64,
                                     hc * 1024 + ch["kcol"]: hc * 1024 + ch["kcol"] + P]
                        qrhs = q_fm[hrow:hrow + 64,
                                    hc * TB + ch["ql"]: hc * TB + ch["qh"]]
                        ev_s = R.op("tensor", (lambda t, klhs=klhs, qrhs=qrhs, bk=bk, N=N: t.matmul(
                            out=bk[:, 0:N], lhsT=klhs, rhs=qrhs, start=True,
                            stop=True)),
                            waits=[bank_free[cc % 4], e_rope_k[hc], e_rope_q[hc]],
                            inc=PE)
                        if ch["ql"] > 0:
                            R.op("gpsimd", (lambda g, slot=slot, ch=ch: g.memset(
                                probs[:, slot * TB: slot * TB + ch["ql"]], 0)),
                                waits=[slot_free[slot]])
                        if ch["qh"] < TB:
                            R.op("gpsimd", (lambda g, slot=slot, ch=ch: g.memset(
                                probs[:, slot * TB + ch["qh"]: (slot + 1) * TB], 0)),
                                waits=[slot_free[slot]])
                        e_z0 = fence("gpsimd", PO)
                        pr = probs[:, slot * TB + ch["ql"]: slot * TB + ch["qh"]]
                        R.op("scalar", (lambda sc, pr=pr, bk=bk, N=N, h=h: sc.activation(
                            out=pr, in_=bk[:, 0:N], func=AF.Exp,
                            bias=negmH[:, h:h + 1])),
                            waits=[ev_s, e_negm, slot_free[slot]], inc=None)
                        ev_e = fence("scalar", SE)
                        bank_free[cc % 4] = ev_e
                        probs_ready = ev_e
                        mq0 = ch["mq0"]
                        if 0 <= mq0 - ch["ql"]:
                            mslice = probs[:, slot * TB + mq0: slot * TB + mq0 + P]
                            tri = lead_sb if ch["lead"] else trail_sb
                            R.op("gpsimd", (lambda g, mslice=mslice, tri=tri: g.tensor_tensor(
                                out=mslice, in0=mslice, in1=tri[:],
                                op=ALU.mult)), waits=[ev_e, e_const], inc=None)
                            probs_ready = fence("gpsimd", PO)
                        cc += 1
                        return dict(slot=slot, ready=probs_ready, z0=e_z0, ch=ch, ci=ci)

                    def emit_consume(h, cx):
                        nonlocal e_om_last
                        ob = outp[h % 2]
                        ch = cx["ch"]
                        ci = cx["ci"]
                        slot = cx["slot"]
                        vlhs = vc_sb[:, ch["vslot"] * 1040 + h * 65:
                                     ch["vslot"] * 1040 + h * 65 + 65]
                        prr = probs[:, slot * TB: (slot + 1) * TB]
                        wl = [cx["ready"], cx["z0"], e_vlast_ve]
                        if ci == 0:
                            wl.append(head_free[h % 2])
                        e_om = R.op("tensor", (lambda t, vlhs=vlhs, prr=prr, ob=ob, ci=ci: t.matmul(
                            out=ob[0:65, :], lhsT=vlhs, rhs=prr, start=(ci == 0),
                            stop=(ci == len(chunks) - 1))),
                            waits=wl, inc=PE)
                        slot_free[slot] = e_om
                        e_om_last = e_om
                        return e_om

                    def emit_tail(h, e_om):
                        nonlocal bcp_free
                        ob = outp[h % 2]
                        if st.get("serecip", True):
                            R.op("vector", (lambda v, ob=ob: v.tensor_copy(
                                out=recip_sb[0:1, 0:TB], in_=ob[64:65, :])),
                                waits=[e_om, st.get("dbgrecip_ev")], inc=None)
                            e_cp = fence("vector", VE)
                            if dbg == "recip" and not st.get("dbg_done2"):
                                st["dbg_done2"] = True
                                st["dbgden_ev"] = R.dma("gpsimd", ydbg[1:2, 0:512],
                                                        recip_sb[0:1, 0:TB], waits=[e_cp])
                            R.op("scalar", lambda sc: sc.activation(
                                out=recip_sb[0:1, TB:2 * TB], in_=recip_sb[0:1, 0:TB],
                                func=AF.Ln), waits=[e_cp], inc=None)
                            e_ln2 = fence("scalar", SE, waits=[st.get("dbgden_ev")] if dbg == "recip" else [])
                            R.op("vector", lambda v: v.tensor_scalar(
                                out=recip_sb[0:1, 0:TB], in0=recip_sb[0:1, TB:2 * TB],
                                scalar1=-1.0, scalar2=None, op0=ALU.mult),
                                waits=[e_ln2], inc=None)
                            e_ng = fence("vector", VE)
                            R.op("scalar", lambda sc: sc.activation(
                                out=recip_sb[0:1, TB:2 * TB], in_=recip_sb[0:1, 0:TB],
                                func=AF.Exp), waits=[e_ng], inc=None)
                            e_recip = fence("scalar", SE)
                        else:
                            R.op("vector", (lambda v, ob=ob: v.reciprocal(
                                out=recip_sb[0:1, 0:TB], in_=ob[64:65, :])),
                                waits=[e_om], inc=None)
                            e_recip = fence("vector", VE)
                        evb = R.op("tensor", (lambda t: t.matmul(
                            out=bcp[0:64, :], lhsT=or_f[0:1, 0:64],
                            rhs=recip_sb[0:1, TB:2 * TB] if st.get("serecip", True) else recip_sb[0:1, 0:TB],
                            start=True, stop=True)),
                            waits=[e_recip, bcp_free], inc=PE)
                        R.op("vector", lambda v: v.tensor_copy(
                            out=bcast_sb[0:64, :], in_=bcp[0:64, :]), waits=[evb], inc=None)
                        e_bc = fence("vector", VE)
                        hrow = (h % 2) * 64
                        hc = h // 2
                        R.op("vector", (lambda v, hrow=hrow, hc=hc, ob=ob: v.tensor_tensor(
                            out=af[hrow:hrow + 64, hc * TB:(hc + 1) * TB],
                            in0=ob[0:64, :], in1=bcast_sb[0:64, :], op=ALU.mult)),
                            waits=[e_bc], inc=None)
                        e_ev = fence("vector", VE)
                        e_pair_ev[h] = e_ev
                        head_free[h % 2] = e_ev
                        bcp_free = e_ev
                        if dbg == "recip" and not st.get("dbg_done"):
                            st["dbg_done"] = True
                            st["dbgrecip_ev"] = R.dma("gpsimd", ydbg[0:1, 0:1024],
                                                      recip_sb[0:1, 0:1024],
                                                      waits=[e_recip])


                    pend = []
                    from collections import deque
                    skew = st.get("skew", 2)
                    for hp in range(0, NH, 2):
                        qs = [deque(), deque()]
                        e_oms = [None, None]
                        for ci in range(len(chunks)):
                            for j in (0, 1):
                                qs[j].append(emit_chunk(hp + j, ci, chunks[ci]))
                            if pend and ci == 1:
                                for pe_ in pend:
                                    emit_tail(*pe_)
                                pend = []
                            for j in (0, 1):
                                if len(qs[j]) > skew:
                                    e_oms[j] = emit_consume(hp + j, qs[j].popleft())
                        for j in (0, 1):
                            while qs[j]:
                                e_oms[j] = emit_consume(hp + j, qs[j].popleft())
                        pend = [(hp, e_oms[0]), (hp + 1, e_oms[1])]
                    for pe_ in pend:
                        emit_tail(*pe_)
                    st["vslot_free"] = e_om_last
                    R.flush()

                with ExitStack() as ps:
                    pse = ps.enter_context
                    o_ps = [pse(nc.psum_tensor(un(f"ops{i}"), [P, TB], F32)) for i in range(2)]
                    gg2 = 0
                    e_res = None
                    e_addw = {}
                    for tc in range(NTC):
                        for half in range(2):
                            wl = [e_wd, e_pair_ev[NH - 1], e_addw.get(gg2 - 2)]
                            evo = None
                            for hcb in range(NFC):
                                lhs = af[:, hcb * TB + tc * P: hcb * TB + (tc + 1) * P]
                                rhsw = woutb[:, hcb * D + half * 512: hcb * D + (half + 1) * 512]
                                evo = R.op("tensor", (lambda t, lhs=lhs, rhsw=rhsw, g=gg2, hcb=hcb: t.matmul(
                                    out=o_ps[g % 2][:], lhsT=lhs, rhs=rhsw,
                                    start=(hcb == 0), stop=(hcb == NFC - 1))),
                                    waits=wl, inc=PE)
                                wl = []
                            xs = x_tm[:, tc * D + half * 512: tc * D + (half + 1) * 512]
                            e_res = R.op("vector", (lambda v, xs=xs, g=gg2: v.tensor_tensor(
                                out=xs, in0=o_ps[g % 2][:], in1=xs,
                                op=ALU.add)), waits=[evo, gate], inc=VE)
                            e_addw[gg2] = e_res
                            gg2 += 1
                    st["x_ready"] = e_res
                    st["woutb_free"] = e_res
                    R.flush()

        for b in range(n_blocks):
            e_xl = R.dma("scalar", x_tm[:].rearrange("p (tc d) -> p tc d", tc=NTC),
                         x_in[b * TB:(b + 1) * TB, :].rearrange("(tc p) d -> p tc d", p=P),
                         waits=[st["x_free"]])
            st["x_ready"] = e_xl
            ffn_phase()
            if n_sub >= 2:
                attn_phase(b)
            if n_sub >= 3:
                ffn_phase()
            e_stw = R.dma("scalar", y[b * TB:(b + 1) * TB, :].rearrange("(tc p) d -> p tc d", p=P),
                          x_tm[:].rearrange("p (tc d) -> p tc d", tc=NTC),
                          waits=[st["x_ready"]])
            st["x_free"] = e_stw
            st["x_ready"] = None
            R.flush()

        R.op("sync", lambda s: s.nop(), waits=[st["x_free"]])
        R.flush()
    return nc


def kernel(L0_attn_x, L0_attn_Wqkv, L0_attn_Wout, L0_ffn_gate_w, L0_ffn_w1,
           L0_ffn_b1, L0_ffn_w2, L0_ffn_b2, L0_ffn_sw1, L0_ffn_sb1,
           L0_ffn_sw2, L0_ffn_sb2):
    x = np.ascontiguousarray(np.asarray(L0_attn_x, dtype=np.float32))
    B, T, Dm = x.shape
    n_cores = 8
    shards = x.reshape(n_cores, (B // n_cores) * T, Dm)
    nc = build_nc()
    ca = np.ascontiguousarray
    w = dict(
        wqkv=ca(np.asarray(L0_attn_Wqkv, np.float32)),
        wout=ca(np.asarray(L0_attn_Wout, np.float32)),
        gate_w=ca(np.asarray(L0_ffn_gate_w, np.float32)),
        w1=ca(np.asarray(L0_ffn_w1, np.float32)),
        b1=ca(np.asarray(L0_ffn_b1, np.float32)),
        w2=ca(np.asarray(L0_ffn_w2, np.float32)),
        b2=ca(np.asarray(L0_ffn_b2, np.float32)),
        sw1=ca(np.asarray(L0_ffn_sw1, np.float32)),
        sb1=ca(np.asarray(L0_ffn_sb1, np.float32)),
        sw2=ca(np.asarray(L0_ffn_sw2, np.float32)),
        sb2=ca(np.asarray(L0_ffn_sb2, np.float32)),
    )
    in_maps = [dict(x_in=ca(shards[c]), **w) for c in range(n_cores)]
    res = run_bass_kernel_spmd(nc, in_maps, core_ids=list(range(n_cores)))
    out = np.concatenate([r["y"] for r in res.results], axis=0)
    return out.reshape(B, T, Dm)
